# version 02
# baseline (speedup 1.0000x reference)
import contextlib
import numpy as np
import concourse.bass as bass
import concourse.mybir as mybir
from concourse.bass_utils import run_bass_kernel_spmd

F32 = mybir.dt.float32
BF16 = mybir.dt.bfloat16
AF = mybir.ActivationFunctionType
ALU = mybir.AluOpType

T = 4096
D = 1024
DC = 8
GT = 512
NG = 8
NTIL = 32
S = 64
DB = 16
CW = T + 32
EPS = 1e-5

_DEBUG_TAPS = []
_STOP_AFTER = None
_P1A_GROUPS = NG if False else 8
_SKIP_DTBC = False
_SKIP_EVAC = False
_P1A_M = 16

TAP_SPECS = {
    "xproj": ([128, DC * CW], BF16), "xconv": ([128, DC * CW], BF16),
    "zdbg": ([128, DC * GT], BF16), "decay": ([S, T], BF16),
    "bhat": ([S, T], BF16), "chat": ([S, T], BF16), "states": ([S, T], BF16),
    "y": ([S, T], BF16), "vfm_dbg": ([128, DC * CW], BF16),
    "xn0": ([128, DC * GT], BF16), "ibx": ([128, 16], F32),
    "r1s": ([128, NTIL], F32), "gath1": ([128, 8 * 17], F32),
    "kact0": ([128, DC * GT], BF16), "xr0": ([128, DC * GT], BF16),
    "xk0": ([128, DC * GT], BF16), "lnst0": ([128, 16], F32),
    "vsc": ([T, D], BF16), "att0": ([128, D], F32), "xm0": ([128, D], F32),
}


_DRY = [False]


class _DryInstr:
    def then_inc(self, *a, **k):
        return self

    def branch_hint(self, *a, **k):
        return self


class _DryQ:
    def __getattr__(self, name):
        return lambda *a, **k: _DryInstr()


class Sem:
    def __init__(self, ctx, nc, name):
        self.h = ctx.enter_context(nc.semaphore(name))
        self.n = 0

    def inc(self, instr, k=1):
        if _DRY[0]:
            self.n += k
            return instr
        instr.then_inc(self.h, k)
        return instr


class Prog:
    ENGS = ("sync", "gpsimd", "tensor", "vector", "scalar")

    def __init__(self):
        self.stream = {e: [] for e in self.ENGS}

    def emit(self, eng, fn):
        _DRY[0] = True
        try:
            fn(_DryQ())
        finally:
            _DRY[0] = False
        self.stream[eng].append(fn)

    def wait(self, eng, sem, val):
        if val > 0:
            self.emit(eng, lambda q, s=sem.h, v=val: q.wait_ge(s, v))


def build_nc():
    nc = bass.Bass("TRN2", target_bir_lowering=False,
                   detect_race_conditions=False)
    ctx = contextlib.ExitStack()

    x_h = nc.dram_tensor("x", [T, D], F32, kind="ExternalInput").ap()
    in_wT = nc.dram_tensor("in_wT", [D, 2 * D], F32, kind="ExternalInput").ap()
    dtbc_wT = nc.dram_tensor("dtbc_wT", [D, 192], F32, kind="ExternalInput").ap()
    out_wT = nc.dram_tensor("out_wT", [D + S + 1, D], F32, kind="ExternalInput").ap()
    wr_h = nc.dram_tensor("wr_T", [D, D], F32, kind="ExternalInput").ap()
    wk_h = nc.dram_tensor("wk_T", [D, D], F32, kind="ExternalInput").ap()
    wv_h = nc.dram_tensor("wv_T", [D, D], F32, kind="ExternalInput").ap()
    vfm_h = nc.dram_tensor("vfm", [128, 98], F32, kind="ExternalInput").ap()
    v64_h = nc.dram_tensor("v64", [S, 8], F32, kind="ExternalInput").ap()
    out_h = nc.dram_tensor("out", [T, D], F32, kind="ExternalOutput").ap()

    xn_sc = nc.dram_tensor("xn_sc", [T, D], BF16, kind="Internal").ap()
    hb_sc = nc.dram_tensor("hb_sc", [2048], F32, kind="Internal").ap()
    v_sc = nc.dram_tensor("v_sc", [T, D], BF16, kind="Internal").ap()
    cc1_in = nc.dram_tensor("cc1_in", [128, 17], F32, kind="Internal").ap()
    cc1_out = nc.dram_tensor("cc1_out", [8, 128, 17], F32, kind="Internal").ap()
    cc2_in = nc.dram_tensor("cc2_in", [128, 8], F32, kind="Internal").ap()
    cc2_out = nc.dram_tensor("cc2_out", [8, 128, 8], F32, kind="Internal").ap()

    taps = {}
    for name in _DEBUG_TAPS:
        shape, dt = TAP_SPECS[name]
        taps[name] = nc.dram_tensor("tap_" + name, shape, dt,
                                    kind="ExternalOutput").ap()

    bfsz, off = {}, 0

    def reg(name, size):
        nonlocal off
        bfsz[name] = (off, size)
        off += size

    reg("XBIG", DC * CW)
    reg("WA", 16384)
    reg("WB", 9216)
    reg("WC", 8 * 192)
    reg("STG", 4096)
    reg("XNFM", 2 * DC * GT)
    reg("D1", 4096)
    reg("BSCR", 8192)
    reg("YEXT", T)
    reg("CBF", 16)
    NBF = off
    bfa = ctx.enter_context(nc.sbuf_tensor("bfa", [128, NBF], BF16))

    def B(name, p0=0, p1=128):
        o, s = bfsz[name]
        return bfa[p0:p1, o:o + s]

    f32sz, off = {}, 0

    def reg2(name, size):
        nonlocal off
        f32sz[name] = (off, size)
        off += size

    reg2("XT", 2 * D)
    reg2("UT", D)
    reg2("AT", 2 * D)
    reg2("XM", 2 * D)
    reg2("SCR", 1024)
    reg2("ST6", 12)
    reg2("AGG", 28)
    reg2("R1S", NTIL)
    reg2("M1S", NTIL)
    reg2("STF", 16)
    reg2("RPH", 24)
    reg2("BND", 35)
    reg2("GTH", 8 * 17)
    reg2("ACC", 8 * 17)
    reg2("CST", 1)
    reg2("VFM", 98)
    reg2("IBX", 16)
    reg2("OMU", 16)
    reg2("A64", 2)
    NF32 = off
    f32a = ctx.enter_context(nc.sbuf_tensor("f32a", [128, NF32], F32))

    def F(name, p0=0, p1=128):
        o, s = f32sz[name]
        return f32a[p0:p1, o:o + s]

    v64 = ctx.enter_context(nc.sbuf_tensor("v64s", [S, 8], F32))
    ps = [ctx.enter_context(nc.psum_tensor("ps%d" % i, [128, 512], F32))
          for i in range(8)]

    s_ldc = Sem(ctx, nc, "s_ldc")
    s_xp = [Sem(ctx, nc, "s_x0"), Sem(ctx, nc, "s_x1")]
    s_wgp = [Sem(ctx, nc, "s_wg0"), Sem(ctx, nc, "s_wg1")]
    s_wc = Sem(ctx, nc, "s_wc")
    s_xnhp = [Sem(ctx, nc, "s_xnh0"), Sem(ctx, nc, "s_xnh1")]
    s_xnf = Sem(ctx, nc, "s_xnf")
    s_vhp = [Sem(ctx, nc, "s_vh0"), Sem(ctx, nc, "s_vh1")]
    s_vf = Sem(ctx, nc, "s_vf")
    s_cc = Sem(ctx, nc, "s_cc")
    s_xmp = [Sem(ctx, nc, "s_xm0"), Sem(ctx, nc, "s_xm1")]
    s_xrp = [Sem(ctx, nc, "s_xr0"), Sem(ctx, nc, "s_xr1")]
    s_tap = Sem(ctx, nc, "s_tap")
    s_pe = Sem(ctx, nc, "s_pe")
    s_ev = Sem(ctx, nc, "s_ev")
    s_a = Sem(ctx, nc, "s_a")
    s_v = Sem(ctx, nc, "s_v")
    s_g = Sem(ctx, nc, "s_g")
    s_finp = [Sem(ctx, nc, "s_fin0"), Sem(ctx, nc, "s_fin1")]

    P = Prog()

    class _StopBuild(Exception):
        pass

    def _maybe_stop(tag):
        if _STOP_AFTER == tag:
            for sm in (s_a, s_v, s_pe, s_g, s_wgp[0], s_wgp[1], s_wc,
                       s_cc, s_tap, s_xnf, s_vf, s_xnhp[0], s_xnhp[1],
                       s_vhp[0], s_vhp[1], s_xmp[0], s_xmp[1],
                       s_finp[0], s_finp[1], s_xp[0], s_xp[1],
                       s_xrp[0], s_xrp[1]):
                P.wait("sync", sm, sm.n)
            raise _StopBuild()

    try:

        xbig3 = B("XBIG").rearrange("p (c w) -> p c w", c=DC)
        wa = B("WA")
        wb = B("WB")
        wc = B("WC")
        stg = B("STG")
        xnfm = B("XNFM")
        d1 = B("D1")
        bscr = B("BSCR")
        yext_all = B("YEXT", 0, 65)
        vfm = F("VFM")
        ibx = F("IBX")
        eps_ap = F("CST")[:, 0:1]

        VC = dict(norm_w=0, norm_b=8, in_b=16, cw0=32, cw1=40, cw2=48, cb=56,
                  dvec=64, mur=72, muk=80, sel=88, gates=96)

        def vcol(name, k=0):
            return vfm[:, VC[name] + k:VC[name] + k + 1]

        pool = B("WB")
        decay = pool[0:64, 0:T]
        bhat = pool[0:64, T:2 * T]
        rawb = bscr[0:64, 0:T]
        rawc = bscr[0:64, T:2 * T]
        chat = bscr[0:64, 0:T]
        states = bscr[0:64, T:2 * T]

        scr1 = F("SCR", 0, 64)[:, 0:512]
        scr2 = F("SCR", 0, 64)[:, 512:1024]
        a_ap = F("A64", 0, 64)[:, 0:1]
        a4_ap = F("A64", 0, 64)[:, 1:2]

        xt = [F("XT")[:, i * D:(i + 1) * D] for i in range(2)]
        ut = F("UT")
        att = [F("AT")[:, i * D:(i + 1) * D] for i in range(2)]
        xmr = [F("XM")[:, i * D:(i + 1) * D] for i in range(2)]
        st6 = F("ST6")
        agg = F("AGG")
        r1s = F("R1S")
        m1s = F("M1S")
        xn_tm = [d1[:, i * D:(i + 1) * D] for i in range(2)]
        xnf = [xnfm[:, i * DC * GT:(i + 1) * DC * GT] for i in range(2)]
        zb = stg

        mu_ap = agg[:, 0:1]
        var_ap = agg[:, 1:2]
        lnv_ap = agg[:, 2:3]
        r_ap = agg[:, 3:4]
        nm_ap = agg[:, 4:5]
        t4a = agg[:, 8:12]
        t4b = agg[:, 12:16]
        t1a = agg[:, 16:17]
        t1b = agg[:, 17:18]

        def ln_stats_tile(v, x_ap):
            v.bn_stats(st6[:, 0:6], x_ap[:, 0:512])
            return v.bn_stats(st6[:, 6:12], x_ap[:, 512:1024])

        def ln_agg_tile(v, mu_out, var_out):
            m = [st6[:, i:i + 1] for i in (1, 4, 7, 10)]
            cv = [st6[:, i:i + 1] for i in (2, 5, 8, 11)]
            Tn = [agg[:, 8 + i:9 + i] for i in range(17)]
            v.tensor_mul(Tn[4], m[0], m[0])
            v.tensor_mul(Tn[5], m[1], m[1])
            v.tensor_mul(Tn[6], m[2], m[2])
            v.tensor_mul(Tn[7], m[3], m[3])
            v.tensor_add(Tn[0], m[0], m[1])
            v.tensor_add(Tn[1], m[2], m[3])
            v.tensor_add(Tn[2], cv[0], cv[1])
            v.tensor_add(Tn[3], cv[2], cv[3])
            v.tensor_add(Tn[8], Tn[4], Tn[5])
            v.tensor_add(Tn[9], Tn[6], Tn[7])
            v.tensor_add(Tn[10], Tn[0], Tn[1])
            v.tensor_add(Tn[11], Tn[2], Tn[3])
            v.tensor_add(Tn[12], Tn[8], Tn[9])
            v.tensor_scalar_mul(Tn[13], Tn[11], 1.0 / 1024.0)
            v.tensor_scalar_mul(mu_out, Tn[10], 0.25)
            v.scalar_tensor_tensor(Tn[15], Tn[12], 0.25, Tn[13],
                                   op0=ALU.mult, op1=ALU.add)
            v.tensor_mul(Tn[14], mu_out, mu_out)
            v.tensor_copy(Tn[16], Tn[8])
            return v.tensor_sub(var_out, Tn[15], Tn[14])

        def rsqrt_chain(s, var_in, r_out):
            s.activation(lnv_ap, var_in, AF.Ln, bias=eps_ap, scale=1.0)
            return s.activation(r_out, lnv_ap, AF.Exp, bias=0.0, scale=-0.5)

        def tap_dump(name, ap, wait_sems):
            if name in taps:
                for sm, v in wait_sems:
                    P.wait("sync", sm, v)
                P.emit("sync", lambda q, name=name, ap=ap: s_tap.inc(
                    q.dma_start(out=taps[name], in_=ap), 16))

        P.emit("sync", lambda q: s_ldc.inc(q.dma_start(out=vfm, in_=vfm_h), 16))
        P.emit("sync", lambda q: s_ldc.inc(q.dma_start(out=v64[:], in_=v64_h), 16))
        ldc_done = s_ldc.n

        P.emit("vector", lambda v: v.memset(eps_ap, EPS))
        P.emit("vector", lambda v: v.memset(yext_all[64:65, :], 1.0))
        def halo_zero(v):
            for c in range(DC):
                v.memset(xbig3[:, c, DB - 1:DB], 0.0)
                li = v.memset(xbig3[:, c, DB + T:DB + T + 1], 0.0)
            return li
        P.emit("vector", lambda v: s_v.inc(halo_zero(v)))
        halos_zeroed = s_v.n

        P.wait("vector", s_ldc, ldc_done)
        P.emit("vector", lambda v: s_v.inc(v.tensor_copy(
            B("CBF")[:, 0:8], vfm[:, VC["norm_b"]:VC["norm_b"] + 8])))
        nb_bf_done = s_v.n
        P.emit("vector", lambda v: v.tensor_scalar(
            F("OMU")[:, 0:8], vfm[:, VC["mur"]:VC["mur"] + 8], -1.0, 1.0,
            op0=ALU.mult, op1=ALU.add))
        P.emit("vector", lambda v: s_v.inc(v.tensor_scalar(
            F("OMU")[:, 8:16], vfm[:, VC["muk"]:VC["muk"] + 8], -1.0, 1.0,
            op0=ALU.mult, op1=ALU.add)))
        omu_done = s_v.n

        def omur(c):
            return F("OMU")[:, c:c + 1]

        def omuk(c):
            return F("OMU")[:, 8 + c:8 + c + 1]

        P.wait("scalar", s_ldc, ldc_done)
        P.emit("scalar", lambda s: s_a.inc(s.activation(
            scr1[:, 0:1], v64[:, 3:4], AF.Exp, bias=0.0, scale=1.0)))
        a_exp_done = s_a.n
        P.wait("vector", s_a, a_exp_done)
        P.emit("vector", lambda v: v.tensor_scalar_mul(a_ap, scr1[:, 0:1], -1.0))
        P.emit("vector", lambda v: s_v.inc(
            v.tensor_scalar_mul(a4_ap, scr1[:, 0:1], -1e-4)))
        a_done = s_v.n

        stage = [stg[:, i * 2048:(i + 1) * 2048] for i in range(2)]
        hb_vals, sc_vals = [], []
        for k in range(DC):
            par = k % 2
            if k >= 2:
                P.wait("gpsimd", s_pe, hb_vals[k - 2])
                P.wait("gpsimd", s_v, sc_vals[k - 2])
            P.emit("gpsimd", lambda g, par=par, k=k: s_wgp[par].inc(
                g.dma_start(out=stage[par], in_=in_wT[k * 128:(k + 1) * 128, :]), 16))
            ld = s_wgp[par].n
            P.wait("tensor", s_wgp[par], ld)
            if k == 0:
                P.wait("tensor", s_ldc, ldc_done)
                P.wait("tensor", s_v, nb_bf_done)

            def hb_mms(t, k=k, par=par):
                for n in range(4):
                    mm = t.matmul(ps[4 + n][0:1, :],
                                  B("CBF")[:, k:k + 1],
                                  stage[par][:, n * 512:(n + 1) * 512],
                                  start=(k == 0), stop=(k == DC - 1))
                    if n == 3:
                        s_pe.inc(mm)
            P.emit("tensor", hb_mms)
            hb_vals.append(s_pe.n)
            P.wait("vector", s_wgp[par], ld)
            P.emit("vector", lambda v, k=k, par=par: s_v.inc(v.tensor_scalar_mul(
                wa[:, k * 2048:(k + 1) * 2048], stage[par], vcol("norm_w", k))))
            sc_vals.append(s_v.n)
        wa_ready = s_v.n
        hb_done = s_pe.n
        hbrow = F("SCR")[0:1, 0:1024]
        P.wait("vector", s_pe, hb_done)
        for half in range(2):
            if half == 1:
                P.wait("vector", s_ldc, s_ldc.n)
            P.emit("vector", lambda v, half=half: v.tensor_copy(
                hbrow[:, 0:512], ps[4 + 2 * half][0:1, :]))
            P.emit("vector", lambda v, half=half: s_v.inc(v.tensor_copy(
                hbrow[:, 512:1024], ps[5 + 2 * half][0:1, :])))
            ev_done = s_v.n
            P.wait("sync", s_v, ev_done)
            P.emit("sync", lambda q, half=half: s_ldc.inc(q.dma_start(
                out=hb_sc[half * 1024:(half + 1) * 1024], in_=hbrow), 16))
        hb_in_hbm = s_ldc.n
        P.wait("sync", s_ldc, hb_in_hbm)
        P.emit("sync", lambda q: s_ldc.inc(q.dma_start(
            out=F("GTH")[:, 0:16],
            in_=hb_sc.rearrange("(m p) -> p m", p=128),
            allow_slow_non_contiguous=True), 16))
        hb_back = s_ldc.n
        P.wait("vector", s_ldc, hb_back)
        P.emit("vector", lambda v: s_v.inc(v.tensor_add(
            ibx, F("GTH")[:, 0:16], vfm[:, VC["in_b"]:VC["in_b"] + 16])))
        ibx_ready = s_v.n

        P.emit("gpsimd", lambda g: s_wc.inc(g.dma_start(
            out=wc.rearrange("p (c d) -> p c d", c=DC),
            in_=dtbc_wT.rearrange("(c p) d -> p c d", p=128)), 16))
        wc_ready = s_wc.n

        tap_dump("ibx", ibx, [(s_v, ibx_ready)])
        _maybe_stop("setup")

        pA = [ps[0], ps[1]]
        pdt, pb_, pc_ = ps[2], ps[3], ps[4]

        xn_done_tile = [0] * NTIL
        xnh_stored = [0] * NTIL
        x_wait = [0] * NTIL
        pe_mm_val = {}
        pe_dtbc_val = [0] * NG
        dt_chain_done = [0] * NG
        xnf_done_g = [0] * NG
        ev_base = s_ev.n
        ng_run = min(NG, _P1A_GROUPS)

        def emit_xload(gi, gate_sem=None, gate_val=0):
            if gate_sem is not None and gate_val > 0:
                P.wait("sync", gate_sem, gate_val)
            P.emit("sync", lambda q, gi=gi: s_xp[gi % 2].inc(q.dma_start(
                out=xt[gi % 2], in_=x_h[gi * 128:(gi + 1) * 128, :]), 16))
            x_wait[gi] = s_xp[gi % 2].n

        emit_xload(0)
        emit_xload(1)

        for g in range(ng_run):
            gs, ge = g * GT, (g + 1) * GT
            for t in range(4):
                gi = 4 * g + t
                P.wait("vector", s_xp[gi % 2], x_wait[gi])
                P.emit("vector", lambda v, gi=gi: ln_stats_tile(v, xt[gi % 2]))
                P.emit("vector", lambda v: s_v.inc(ln_agg_tile(v, mu_ap, var_ap)))
                ln1_done = s_v.n
                if gi == 0 and "lnst0" in taps:
                    def snap0(v):
                        v.tensor_copy(F("STF")[:, 0:12], st6[:, 0:12])
                        v.tensor_copy(F("STF")[:, 12:13], mu_ap)
                        return v.tensor_copy(F("STF")[:, 13:14], var_ap)
                    P.emit("vector", lambda v: s_v.inc(snap0(v)))
                    snap_done = s_v.n
                    P.wait("sync", s_v, snap_done)
                    P.emit("sync", lambda q: s_tap.inc(q.dma_start(
                        out=taps["lnst0"], in_=F("STF")[:, 0:16]), 16))
                P.wait("scalar", s_v, ln1_done)
                P.emit("scalar", lambda s, gi=gi: s_a.inc(
                    rsqrt_chain(s, var_ap, r1s[:, gi:gi + 1])))
                r1_done = s_a.n
                P.wait("vector", s_a, r1_done)
                P.emit("vector", lambda v, gi=gi: s_v.inc(v.tensor_scalar(
                    m1s[:, gi:gi + 1], mu_ap, r1s[:, gi:gi + 1], -1.0,
                    op0=ALU.mult, op1=ALU.mult)))
                nm1_done = s_v.n
                P.wait("scalar", s_v, nm1_done)
                P.emit("scalar", lambda s, gi=gi: s_a.inc(s.activation(
                    ut, xt[gi % 2], AF.Identity,
                    bias=m1s[:, gi:gi + 1], scale=r1s[:, gi:gi + 1])))
                u_done = s_a.n
                P.wait("vector", s_a, u_done)
                P.emit("vector", lambda v: ln_stats_tile(v, ut))
                P.emit("vector", lambda v: s_v.inc(ln_agg_tile(v, mu_ap, var_ap)))
                ln2_done = s_v.n
                P.wait("scalar", s_v, ln2_done)
                P.emit("scalar", lambda s: s_a.inc(
                    rsqrt_chain(s, var_ap, r_ap)))
                r2_done = s_a.n
                P.wait("vector", s_a, r2_done)
                P.emit("vector", lambda v: s_v.inc(v.tensor_scalar(
                    nm_ap, mu_ap, r_ap, -1.0, op0=ALU.mult, op1=ALU.mult)))
                nm2_done = s_v.n
                P.wait("scalar", s_v, nm2_done)
                if gi >= 2:
                    P.wait("scalar", s_xnhp[gi % 2], xnh_stored[gi - 2])
                P.emit("scalar", lambda s, gi=gi: s_a.inc(s.activation(
                    xn_tm[gi % 2], ut, AF.Identity,
                    bias=nm_ap, scale=r_ap)))
                xn_done_tile[gi] = s_a.n
                P.wait("sync", s_a, xn_done_tile[gi])
                P.emit("sync", lambda q, gi=gi: s_xnhp[gi % 2].inc(q.dma_start(
                    out=xn_sc[gi * 128:(gi + 1) * 128, :], in_=xn_tm[gi % 2]), 16))
                xnh_stored[gi] = s_xnhp[gi % 2].n
                if gi + 2 < NTIL:
                    emit_xload(gi + 2)

            _maybe_stop("p1a_ln%d" % g)
            if g >= 2:
                P.wait("sync", s_pe, pe_dtbc_val[g - 2])
            P.wait("sync", s_xnhp[0], s_xnhp[0].n)
            P.wait("sync", s_xnhp[1], s_xnhp[1].n)
            for c in range(DC):
                P.emit("sync", lambda q, g=g, c=c, gs=gs, ge=ge: s_xnf.inc(
                    q.dma_start_transpose(
                        xnf[g % 2][:, c * GT:(c + 1) * GT],
                        xn_sc[gs:ge, c * 128:(c + 1) * 128]), 16))
            xnf_done_g[g] = s_xnf.n

            if g == 0:
                tap_dump("xn0", xnf[0], [(s_xnf, xnf_done_g[0])])

            _maybe_stop("p1a_tr%d" % g)
            P.wait("tensor", s_xnf, xnf_done_g[g])
            if g == 0:
                P.wait("tensor", s_v, wa_ready)
                P.wait("tensor", s_wc, wc_ready)
            for m in range(_P1A_M):
                need = ev_base + _P1A_M * g + m - 1
                if need > ev_base and not _SKIP_EVAC:
                    P.wait("tensor", s_ev, need)

                def mm_in(tt, g=g, m=m):
                    for k in range(DC):
                        mm = tt.matmul(pA[m % 2][:, :],
                                       wa[:, k * 2048 + m * 128:
                                          k * 2048 + (m + 1) * 128],
                                       xnf[g % 2][:, k * GT:(k + 1) * GT],
                                       start=(k == 0), stop=(k == DC - 1))
                        if k == DC - 1:
                            s_pe.inc(mm)
                P.emit("tensor", mm_in)
                pe_mm_val[(g, m)] = s_pe.n
            P.wait("tensor", s_ev, ev_base + _P1A_M * g + _P1A_M)
            if g >= 1 and not _SKIP_DTBC:
                P.wait("tensor", s_a, dt_chain_done[g - 1])

            def mm_dtbc(tt, g=g):
                for bi, pp in enumerate((pdt, pb_, pc_)):
                    for k in range(DC):
                        mm = tt.matmul(pp[0:64, :],
                                       wc[:, k * 192 + bi * 64:
                                          k * 192 + (bi + 1) * 64],
                                       zb[:, k * GT:(k + 1) * GT],
                                       start=(k == 0), stop=(k == DC - 1))
                        if bi == 2 and k == DC - 1:
                            s_pe.inc(mm)
            P.emit("tensor", mm_dtbc)
            pe_dtbc_val[g] = s_pe.n

            _maybe_stop("p1a_mm%d" % g)
            if _SKIP_EVAC:
                P.wait("scalar", s_pe, pe_mm_val[(g, 15)])
                dt_chain_done[g] = s_a.n
                continue
            if g == 0:
                P.wait("scalar", s_v, ibx_ready)
            for m in range(_P1A_M):
                P.wait("scalar", s_pe, pe_mm_val[(g, m)])
                if m < 8:
                    dst = xbig3[:, m, DB + gs:DB + ge]
                else:
                    dst = zb[:, (m - 8) * GT:(m - 7) * GT]
                P.emit("scalar", lambda s, m=m, dst=dst: s_ev.inc(s.activation(
                    dst, pA[m % 2][:, :], AF.Identity,
                    bias=ibx[:, m:m + 1], scale=1.0)))
            if _SKIP_DTBC:
                dt_chain_done[g] = s_a.n
                continue
            P.wait("scalar", s_pe, pe_dtbc_val[g])
            if g == 0:
                P.wait("scalar", s_v, a_done)
            P.emit("scalar", lambda s: s.activation(
                scr1, pdt[0:64, :], AF.Exp, bias=v64[:, 0:1], scale=1.0))
            P.emit("scalar", lambda s: s.activation(
                scr2, scr1, AF.Ln, bias=1.0, scale=1.0))
            P.emit("scalar", lambda s, gs=gs, ge=ge: s.activation(
                decay[:, gs:ge], scr2, AF.Exp, bias=a4_ap, scale=a_ap))
            P.emit("scalar", lambda s, gs=gs, ge=ge: s.activation(
                rawb[:, gs:ge], pb_[0:64, :], AF.Identity,
                bias=v64[:, 1:2], scale=1.0))
            P.emit("scalar", lambda s, gs=gs, ge=ge: s_a.inc(s.activation(
                rawc[:, gs:ge], pc_[0:64, :], AF.Identity,
                bias=v64[:, 2:3], scale=1.0)))
            dt_chain_done[g] = s_a.n

        p1a_act_done = s_a.n
        _maybe_stop("p1a")
        tap_dump("xproj", B("XBIG"), [(s_a, p1a_act_done)])
        tap_dump("zdbg", zb, [(s_a, p1a_act_done)])
        tap_dump("r1s", r1s, [(s_a, p1a_act_done)])

        P.emit("scalar", lambda s: s.activation(bhat, rawb, AF.Tanh,
                                                bias=0.0, scale=1.0))
        P.emit("scalar", lambda s: s_a.inc(s.activation(
            chat, rawc, AF.Tanh, bias=0.0, scale=1.0)))
        tanh_bc_done = s_a.n

        P.wait("vector", s_a, tanh_bc_done)
        P.emit("vector", lambda v: s_v.inc(v.tensor_tensor_scan(
            states, decay, bhat, 0.0, op0=ALU.mult, op1=ALU.add)))
        scan1_done = s_v.n

        bnd = F("BND")[:, 0:17]
        P.emit("vector", lambda v: v.memset(bnd, 0.0))
        def bnd_fill(v):
            for c in range(DC):
                v.tensor_copy(bnd[:, c:c + 1], xbig3[:, c, DB:DB + 1])
                v.tensor_copy(bnd[:, 8 + c:9 + c],
                              xbig3[:, c, DB + T - 1:DB + T])
        P.emit("vector", bnd_fill)
        P.emit("vector", lambda v: s_v.inc(v.tensor_copy(
            bnd[0:64, 16:17], states[:, T - 1:T])))
        bnd_done = s_v.n
        P.wait("sync", s_v, bnd_done)
        P.emit("sync", lambda q: s_cc.inc(q.dma_start(out=cc1_in, in_=bnd), 16))
        cc1_staged = s_cc.n
        P.wait("gpsimd", s_cc, cc1_staged)
        P.emit("gpsimd", lambda g: s_g.inc(g.collective_compute(
            "AllGather", ALU.bypass, replica_groups=[list(range(8))],
            ins=[cc1_in], outs=[cc1_out])))
        cc1_done = s_g.n
        P.wait("sync", s_g, cc1_done)
        P.emit("sync", lambda q: s_cc.inc(q.dma_start(
            out=F("GTH").rearrange("p (r k) -> p r k", r=8),
            in_=cc1_out.rearrange("r p k -> p r k")), 16))
        gth1_done = s_cc.n
        tap_dump("gath1", F("GTH"), [(s_cc, gth1_done)])

        cw0 = [vcol("cw0", c) for c in range(DC)]
        cw1 = [vcol("cw1", c) for c in range(DC)]
        cw2 = [vcol("cw2", c) for c in range(DC)]
        cpre = [xnfm[:, i * DC * GT:(i + 1) * DC * GT] for i in range(2)]
        sv0 = F("BND")[:, 18:26]
        sv1 = F("BND")[:, 26:34]
        taps_done = [0] * NG
        tanh_done = [0] * NG
        for g in range(NG):
            gs, ge = g * GT, (g + 1) * GT
            if g == 0:
                P.wait("vector", s_a, p1a_act_done)
                P.wait("vector", s_v, halos_zeroed)
            if g >= 2:
                P.wait("vector", s_a, tanh_done[g - 2])

            def conv_taps(v, g=g, gs=gs, ge=ge):
                buf = cpre[g % 2]
                for c in range(DC):
                    b = buf[:, c * GT:(c + 1) * GT]
                    v.tensor_scalar_mul(b, xbig3[:, c, DB - 1 + gs:DB - 1 + ge],
                                        cw0[c])
                    v.scalar_tensor_tensor(b, xbig3[:, c, DB + gs:DB + ge],
                                           cw1[c], b, op0=ALU.mult, op1=ALU.add)
                    v.scalar_tensor_tensor(b, xbig3[:, c, DB + 1 + gs:DB + 1 + ge],
                                           cw2[c], b, op0=ALU.mult, op1=ALU.add)
            P.emit("vector", conv_taps)
            if g == 0:
                def sv0_fill(v):
                    for c in range(DC):
                        li = v.tensor_copy(sv0[:, c:c + 1],
                                           cpre[0][:, c * GT:c * GT + 1])
                    return li
                P.emit("vector", lambda v: s_v.inc(sv0_fill(v)))
            elif g == NG - 1:
                def sv1_fill(v):
                    for c in range(DC):
                        li = v.tensor_copy(
                            sv1[:, c:c + 1],
                            cpre[(NG - 1) % 2][:, c * GT + GT - 1:c * GT + GT])
                    return li
                P.emit("vector", lambda v: s_v.inc(sv1_fill(v)))
            else:
                P.emit("vector", lambda v, g=g: s_v.inc(v.tensor_copy(
                    agg[:, 26:27], cpre[g % 2][:, 0:1])))
            taps_done[g] = s_v.n
            if g >= 1:
                gp = g - 1
                P.wait("scalar", s_v, taps_done[g])

                def tanh_g(s, gp=gp):
                    for c in range(DC):
                        ai = s.activation(
                            xbig3[:, c, DB + gp * GT:DB + (gp + 1) * GT],
                            cpre[gp % 2][:, c * GT:(c + 1) * GT],
                            AF.Tanh, bias=vcol("cb", c), scale=1.0)
                        if c == DC - 1:
                            s_a.inc(ai)
                P.emit("scalar", tanh_g)
                tanh_done[gp] = s_a.n
        P.wait("scalar", s_v, taps_done[NG - 1])

        def tanh_last(s):
            gp = NG - 1
            for c in range(DC):
                ai = s.activation(xbig3[:, c, DB + gp * GT:DB + (gp + 1) * GT],
                                  cpre[gp % 2][:, c * GT:(c + 1) * GT],
                                  AF.Tanh, bias=vcol("cb", c), scale=1.0)
                if c == DC - 1:
                    s_a.inc(ai)
        P.emit("scalar", tanh_last)
        tanh_done[NG - 1] = s_a.n

        acc = [F("ACC")[:, 0:17], F("ACC")[:, 17:34]]
        sel = vfm[:, VC["sel"]:VC["sel"] + 8]
        g_left = vfm[:, VC["gates"]:VC["gates"] + 1]
        g_right = vfm[:, VC["gates"] + 1:VC["gates"] + 2]
        gth = F("GTH")
        P.wait("vector", s_cc, gth1_done)
        P.wait("vector", s_a, tanh_done[NG - 1])

        PR = [F("ACC")[:, i * 17:(i + 1) * 17] for i in range(8)]

        def select_tree(v, width):
            for r in range(8):
                v.tensor_scalar_mul(PR[r][:, 0:width],
                                    gth[:, r * width:(r + 1) * width],
                                    sel[:, r:r + 1])
            for r in range(4):
                v.tensor_add(PR[2 * r][:, 0:width], PR[2 * r][:, 0:width],
                             PR[2 * r + 1][:, 0:width])
            v.tensor_add(PR[0][:, 0:width], PR[0][:, 0:width],
                         PR[2][:, 0:width])
            v.tensor_add(PR[4][:, 0:width], PR[4][:, 0:width],
                         PR[6][:, 0:width])
            v.tensor_copy(PR[3][:, 0:width], PR[2][:, 0:width])
            v.tensor_add(PR[1][:, 0:width], PR[0][:, 0:width],
                         PR[4][:, 0:width])
            return v.tensor_copy(PR[2][:, 0:width], PR[0][:, 0:width])
        P.emit("vector", lambda v: select_tree(v, 17))
        recv = PR[1]
        def halo_fill(v):
            for c in range(DC):
                v.tensor_scalar_mul(xbig3[:, c, DB - 1:DB],
                                    recv[:, 8 + c:9 + c], g_left)
                v.tensor_scalar_mul(xbig3[:, c, DB + T:DB + T + 1],
                                    recv[:, c:c + 1], g_right)
        P.emit("vector", halo_fill)
        carry = F("BND")[0:64, 34:35]
        P.emit("vector", lambda v: s_v.inc(v.tensor_scalar_mul(
            carry, recv[0:64, 16:17], g_left[0:64, :])))
        ex1_done = s_v.n

        fix0 = F("GTH")[:, 0:8]
        fix1 = F("GTH")[:, 8:16]

        def fixup_pre(v):
            for c in range(DC):
                v.scalar_tensor_tensor(fix0[:, c:c + 1],
                                       xbig3[:, c, DB - 1:DB],
                                       cw0[c], sv0[:, c:c + 1],
                                       op0=ALU.mult, op1=ALU.add)
                li = v.scalar_tensor_tensor(fix1[:, c:c + 1],
                                            xbig3[:, c, DB + T:DB + T + 1],
                                            cw2[c], sv1[:, c:c + 1],
                                            op0=ALU.mult, op1=ALU.add)
            return li
        def fixup_pre_inc(v):
            s_v.inc(fixup_pre(v))
        P.emit("vector", fixup_pre_inc)
        fixpre_done = s_v.n
        P.wait("scalar", s_v, fixpre_done)

        def fixup_tanh(s):
            for c in range(DC):
                s.activation(xbig3[:, c, DB:DB + 1], fix0[:, c:c + 1], AF.Tanh,
                             bias=vcol("cb", c), scale=1.0)
                ai = s.activation(xbig3[:, c, DB + T - 1:DB + T],
                                  fix1[:, c:c + 1],
                                  AF.Tanh, bias=vcol("cb", c), scale=1.0)
                if c == DC - 1:
                    s_a.inc(ai)
        P.emit("scalar", fixup_tanh)
        conv_all_done = s_a.n

        P.wait("vector", s_v, ex1_done)
        P.emit("vector", lambda v: v.tensor_tensor_scan(
            states, decay, bhat, carry[:, 0:1], op0=ALU.mult, op1=ALU.add))
        yext = yext_all[0:64, :]
        P.emit("vector", lambda v: s_v.inc(v.tensor_mul(yext, chat, states)))
        y_done = s_v.n

        for nm, apv, sm, vv in (("decay", decay, s_a, tanh_bc_done),
                                ("bhat", bhat, s_a, tanh_bc_done),
                                ("chat", chat, s_a, tanh_bc_done),
                                ("states", states, s_v, y_done),
                                ("y", yext, s_v, y_done)):
            tap_dump(nm, apv, [(sm, vv)])
        tap_dump("xconv", B("XBIG"), [(s_a, conv_all_done), (s_v, y_done)])

        owt_stage = [bscr[:, i * 1024:(i + 1) * 1024] for i in range(2)]
        P.wait("gpsimd", s_v, y_done)
        P.wait("gpsimd", s_a, tanh_bc_done)
        owt_scaled = [0] * 9
        for k in range(9):
            par = k % 2
            if k >= 2:
                P.wait("gpsimd", s_v, owt_scaled[k - 2])
            if k < 8:
                P.emit("gpsimd", lambda g, k=k, par=par: s_wgp[par].inc(g.dma_start(
                    out=owt_stage[par], in_=out_wT[k * 128:(k + 1) * 128, :]), 16))
            else:
                P.emit("gpsimd", lambda g, par=par: s_wgp[par].inc(g.dma_start(
                    out=owt_stage[par][0:65, :], in_=out_wT[1024:1089, :]), 16))
            ldv = s_wgp[par].n
            P.wait("vector", s_wgp[par], ldv)
            if k < 8:
                P.emit("vector", lambda v, k=k, par=par: s_v.inc(v.tensor_scalar_mul(
                    wb[:, k * 1024:(k + 1) * 1024], owt_stage[par], vcol("dvec", k))))
            else:
                P.emit("vector", lambda v, par=par: s_v.inc(v.tensor_copy(
                    wb[0:65, 8 * 1024:9 * 1024], owt_stage[par][0:65, :])))
            owt_scaled[k] = s_v.n
        owt_ready = s_v.n
        _maybe_stop("p1c")

        vtm = [d1[:, i * D:(i + 1) * D] for i in range(2)]
        x2_wait = [0] * NTIL
        vtm_done = [0] * NTIL
        vtm_stored = [0] * NTIL
        psfree_val = [0] * NTIL
        lnAap_tile = [0] * NTIL
        xm_done_tile = [0] * NTIL
        xm_stored = [0] * NTIL
        pe2_val = [0] * NTIL
        pe2_group = [0] * NG
        vf_done_g = [0] * NG

        def emit_x2load(gi):
            P.emit("sync", lambda q, gi=gi: s_xp[gi % 2].inc(q.dma_start(
                out=xt[gi % 2], in_=x_h[gi * 128:(gi + 1) * 128, :]), 16))
            x2_wait[gi] = s_xp[gi % 2].n

        emit_x2load(0)
        emit_x2load(1)

        for g in range(NG):
            gs = g * GT
            for t in range(4):
                gi = 4 * g + t
                tw0 = gs + t * 128
                pn = [ps[2 * (gi % 2)], ps[2 * (gi % 2) + 1]]

                if gi == 0:
                    P.wait("tensor", s_v, owt_ready)
                    P.wait("tensor", s_a, conv_all_done)
                    P.wait("tensor", s_v, y_done)
                if gi >= 2:
                    P.wait("tensor", s_v, psfree_val[gi - 2])

                def mm_out(tt, tw0=tw0, pn=pn):
                    for k in range(9):
                        if k < 8:
                            lhs = xbig3[:, k, DB + tw0:DB + tw0 + 128]
                            r0, r1_ = 0, 128
                        else:
                            lhs = yext_all[0:65, tw0:tw0 + 128]
                            r0, r1_ = 0, 65
                        for n in range(2):
                            mm = tt.matmul(pn[n][:, :], lhs,
                                           wb[r0:r1_, k * 1024 + n * 512:
                                              k * 1024 + (n + 1) * 512],
                                           start=(k == 0), stop=(k == 8))
                            if k == 8 and n == 1:
                                s_pe.inc(mm)
                P.emit("tensor", mm_out)
                pe2_val[gi] = s_pe.n
                if t == 3:
                    pe2_group[g] = s_pe.n

                P.wait("scalar", s_xp[gi % 2], x2_wait[gi])
                if gi >= 1:
                    P.wait("scalar", s_v, xm_done_tile[gi - 1])
                P.emit("scalar", lambda s, gi=gi: s_a.inc(s.activation(
                    ut, xt[gi % 2], AF.Identity,
                    bias=m1s[:, gi:gi + 1], scale=r1s[:, gi:gi + 1])))
                u2_done = s_a.n
                P.wait("vector", s_pe, pe2_val[gi])
                P.wait("vector", s_a, u2_done)
                if gi >= 2:
                    P.wait("vector", s_a, lnAap_tile[gi - 2])
                P.emit("vector", lambda v, gi=gi, pn=pn: v.tensor_add(
                    att[gi % 2][:, 0:512], pn[0][:, :], ut[:, 0:512]))
                P.emit("vector", lambda v, gi=gi, pn=pn: s_v.inc(v.tensor_add(
                    att[gi % 2][:, 512:1024], pn[1][:, :], ut[:, 512:1024])))
                psfree_val[gi] = s_v.n
                if gi == 0 and "att0" in taps:
                    P.wait("sync", s_v, psfree_val[0])
                    P.emit("sync", lambda q: s_tap.inc(q.dma_start(
                        out=taps["att0"], in_=att[0]), 16))
                P.emit("vector", lambda v, gi=gi: ln_stats_tile(v, att[gi % 2]))
                P.emit("vector", lambda v: s_v.inc(ln_agg_tile(v, mu_ap, var_ap)))
                lnA_done = s_v.n
                P.wait("scalar", s_v, lnA_done)
                P.emit("scalar", lambda s: s_a.inc(
                    rsqrt_chain(s, var_ap, r_ap)))
                rA_done = s_a.n
                P.wait("vector", s_a, rA_done)
                P.emit("vector", lambda v: s_v.inc(v.tensor_scalar(
                    nm_ap, mu_ap, r_ap, -1.0, op0=ALU.mult, op1=ALU.mult)))
                nmA_done = s_v.n
                P.wait("scalar", s_v, nmA_done)
                P.emit("scalar", lambda s, gi=gi: s_a.inc(s.activation(
                    ut, att[gi % 2], AF.Identity,
                    bias=nm_ap, scale=r_ap)))
                lnAap_tile[gi] = s_a.n
                P.wait("vector", s_a, lnAap_tile[gi])
                if gi >= 2:
                    P.wait("vector", s_a, vtm_done[gi - 2])
                    P.wait("vector", s_xmp[gi % 2], xm_stored[gi - 2])
                P.emit("vector", lambda v, gi=gi: s_v.inc(v.tensor_add(
                    xmr[gi % 2], xt[gi % 2], ut)))
                xm_done_tile[gi] = s_v.n
                if gi == 0 and "xm0" in taps:
                    P.wait("sync", s_v, xm_done_tile[0])
                    P.emit("sync", lambda q: s_tap.inc(q.dma_start(
                        out=taps["xm0"], in_=xmr[0]), 16))
                P.wait("sync", s_v, xm_done_tile[gi])
                P.emit("sync", lambda q, gi=gi: s_xmp[gi % 2].inc(q.dma_start(
                    out=out_h[gi * 128:(gi + 1) * 128, :], in_=xmr[gi % 2]), 16))
                xm_stored[gi] = s_xmp[gi % 2].n
                P.emit("vector", lambda v, gi=gi: ln_stats_tile(v, xmr[gi % 2]))
                P.emit("vector", lambda v: s_v.inc(ln_agg_tile(v, mu_ap, var_ap)))
                ln3_done = s_v.n
                P.wait("scalar", s_v, ln3_done)
                P.emit("scalar", lambda s: s_a.inc(
                    rsqrt_chain(s, var_ap, r_ap)))
                r3_done = s_a.n
                P.wait("vector", s_a, r3_done)
                P.emit("vector", lambda v: s_v.inc(v.tensor_scalar(
                    nm_ap, mu_ap, r_ap, -1.0, op0=ALU.mult, op1=ALU.mult)))
                nm3_done = s_v.n
                P.wait("scalar", s_v, nm3_done)
                if gi >= 2:
                    P.wait("scalar", s_vhp[gi % 2], vtm_stored[gi - 2])
                P.emit("scalar", lambda s, gi=gi: s_a.inc(s.activation(
                    vtm[gi % 2], xmr[gi % 2], AF.Identity,
                    bias=nm_ap, scale=r_ap)))
                vtm_done[gi] = s_a.n
                P.wait("sync", s_a, vtm_done[gi])
                P.emit("sync", lambda q, gi=gi: s_vhp[gi % 2].inc(q.dma_start(
                    out=v_sc[gi * 128:(gi + 1) * 128, :], in_=vtm[gi % 2]), 16))
                vtm_stored[gi] = s_vhp[gi % 2].n
                if gi + 2 < NTIL:
                    emit_x2load(gi + 2)

            P.wait("sync", s_pe, pe2_group[g])
            P.wait("sync", s_vhp[0], s_vhp[0].n)
            P.wait("sync", s_vhp[1], s_vhp[1].n)
            for c in range(DC):
                P.emit("sync", lambda q, c=c, gs=gs: s_vf.inc(
                    q.dma_start_transpose(
                        B("XBIG")[:, c * CW + DB + gs:c * CW + DB + gs + GT],
                        v_sc[gs:gs + GT, c * 128:(c + 1) * 128]), 16))
            vf_done_g[g] = s_vf.n

        if "vsc" in taps:
            P.wait("sync", s_vhp[0], s_vhp[0].n)
            P.wait("sync", s_vhp[1], s_vhp[1].n)
            P.emit("sync", lambda q: s_tap.inc(
                q.dma_start(out=taps["vsc"], in_=v_sc), 16))
        P.wait("gpsimd", s_pe, pe2_group[0])
        P.emit("gpsimd", lambda g: s_wgp[0].inc(g.dma_start(
            out=wa[:, 0:8192].rearrange("p (c d) -> p c d", c=DC),
            in_=wr_h.rearrange("(c p) d -> p c d", p=128)), 16))
        P.emit("gpsimd", lambda g: s_wgp[1].inc(g.dma_start(
            out=wa[:, 8192:16384].rearrange("p (c d) -> p c d", c=DC),
            in_=wk_h.rearrange("(c p) d -> p c d", p=128)), 16))
        wrk_ready = (s_wgp[0].n, s_wgp[1].n)

        bnd2 = F("BND")[:, 0:8]
        P.wait("vector", s_vf, vf_done_g[NG - 1])
        def bnd2_fill(v):
            for c in range(DC):
                li = v.tensor_copy(bnd2[:, c:c + 1],
                                   xbig3[:, c, DB + T - 1:DB + T])
            return li
        P.emit("vector", lambda v: s_v.inc(bnd2_fill(v)))
        bnd2_done = s_v.n
        P.wait("sync", s_v, bnd2_done)
        P.emit("sync", lambda q: s_cc.inc(q.dma_start(out=cc2_in, in_=bnd2), 16))
        cc2_staged = s_cc.n
        P.wait("gpsimd", s_cc, cc2_staged)
        P.emit("gpsimd", lambda g: s_g.inc(g.collective_compute(
            "AllGather", ALU.bypass, replica_groups=[list(range(8))],
            ins=[cc2_in], outs=[cc2_out])))
        cc2_done = s_g.n
        P.wait("sync", s_g, cc2_done)
        P.emit("sync", lambda q: s_cc.inc(q.dma_start(
            out=F("GTH")[:, 0:64].rearrange("p (r k) -> p r k", r=8),
            in_=cc2_out.rearrange("r p k -> p r k")), 16))
        gth2_done = s_cc.n
        P.wait("vector", s_cc, gth2_done)

        P.emit("vector", lambda v: select_tree(v, 8))
        def vprev_fill(v):
            for c in range(DC):
                li = v.tensor_scalar_mul(xbig3[:, c, DB - 1:DB],
                                         recv[:, c:c + 1], g_left)
            return li
        P.emit("vector", lambda v: s_v.inc(vprev_fill(v)))
        ex2_done = s_v.n
        tap_dump("vfm_dbg", B("XBIG"), [(s_v, ex2_done)])
        _maybe_stop("p2a")

        P.wait("gpsimd", s_pe, pe2_group[NG - 1])
        P.emit("gpsimd", lambda g: s_wc.inc(g.dma_start(
            out=wb[:, 0:8192].rearrange("p (c d) -> p c d", c=DC),
            in_=wv_h.rearrange("(c p) d -> p c d", p=128)), 16))
        wv_ready = s_wc.n

        xrb = xnfm[:, 0:DC * GT]
        xkb = xnfm[:, DC * GT:2 * DC * GT]
        s1b = bscr[:, 0:512]
        krel = [bscr[:, 512 + i * 512:512 + (i + 1) * 512] for i in range(2)]
        rr = [bscr[:, 1536 + i * D:1536 + (i + 1) * D] for i in range(2)]
        accsc = bscr[:, 3584:4608]
        kact = B("YEXT")
        ffsl = [stg, d1]
        st6f_mu = F("STF")[:, 0:8]
        st6f_var = F("STF")[:, 8:16]
        rph = F("RPH")
        mur = [vcol("mur", c) for c in range(DC)]
        muk = [vcol("muk", c) for c in range(DC)]

        out_stored = [0] * NTIL
        xm_reread = [0] * NTIL
        pe_k_val = {}
        pe_r_val = {}
        pe_v_val = {}
        rr_done = {}
        krel_val = {}
        kact_done = {}
        dve_ffn_val = {}
        acc_done = {}
        p2d_outv_last = 0

        for p in range(4):
            for gl, g in enumerate((2 * p, 2 * p + 1)):
                gs = g * GT
                if g == 0:
                    P.wait("vector", s_v, ex2_done)
                    P.wait("vector", s_v, omu_done)
                else:
                    P.wait("vector", s_vf, vf_done_g[g])
                    P.wait("vector", s_pe, pe_v_val[4 * (g - 1) + 3])

                def mk_xrk(v, g=g, gs=gs):
                    for c in range(DC):
                        vwin = xbig3[:, c, DB + gs:DB + gs + GT]
                        vprev = xbig3[:, c, DB - 1 + gs:DB - 1 + gs + GT]
                        v.tensor_scalar_mul(s1b, vwin, mur[c])
                        v.scalar_tensor_tensor(xrb[:, c * GT:(c + 1) * GT],
                                               vprev, omur(c), s1b,
                                               op0=ALU.mult, op1=ALU.add)
                        v.tensor_scalar_mul(s1b, vwin, muk[c])
                        li = v.scalar_tensor_tensor(xkb[:, c * GT:(c + 1) * GT],
                                                    vprev, omuk(c), s1b,
                                                    op0=ALU.mult, op1=ALU.add)
                        if c == DC - 1:
                            s_v.inc(li)
                P.emit("vector", mk_xrk)
                xrk_done = s_v.n
                if g == 0:
                    tap_dump("xr0", xrb, [(s_v, xrk_done)])
                    tap_dump("xk0", xkb, [(s_v, xrk_done)])

                P.wait("tensor", s_v, xrk_done)
                if g == 0:
                    P.wait("tensor", s_wgp[0], wrk_ready[0])
                    P.wait("tensor", s_wgp[1], wrk_ready[1])
                for m in range(DC):
                    prev = (g, m - 2) if m >= 2 else (
                        (g - 1, DC - 2 + m) if g >= 1 else None)
                    if prev is not None and prev in krel_val:
                        P.wait("tensor", s_a, krel_val[prev])

                    def mm_k(tt, m=m):
                        pp = ps[4 + m % 2]
                        for k in range(DC):
                            mm = tt.matmul(pp[:, :],
                                           wa[:, 8192 + k * 1024 + m * 128:
                                              8192 + k * 1024 + (m + 1) * 128],
                                           xkb[:, k * GT:(k + 1) * GT],
                                           start=(k == 0), stop=(k == DC - 1))
                            if k == DC - 1:
                                s_pe.inc(mm)
                    P.emit("tensor", mm_k)
                    pe_k_val[(g, m)] = s_pe.n
                    P.wait("scalar", s_pe, pe_k_val[(g, m)])
                    if prev is not None and prev in kact_done:
                        P.wait("scalar", s_v, kact_done[prev])
                    P.emit("scalar", lambda s, m=m: s_a.inc(s.activation(
                        krel[m % 2], ps[4 + m % 2][:, :], AF.Relu,
                        bias=0.0, scale=1.0)))
                    krel_val[(g, m)] = s_a.n
                    P.wait("vector", s_a, krel_val[(g, m)])
                    if m == 0 and g >= 1:
                        P.wait("vector", s_pe, pe_v_val[4 * (g - 1) + 3])
                    P.emit("vector", lambda v, m=m: s_v.inc(v.tensor_mul(
                        kact[:, m * GT:(m + 1) * GT], krel[m % 2], krel[m % 2])))
                    kact_done[(g, m)] = s_v.n
                if g == 0:
                    tap_dump("kact0", kact, [(s_v, kact_done[(0, DC - 1)])])

                for t in range(4):
                    gi = 4 * g + t
                    jl = 4 * gl + t
                    if gi >= 1:
                        P.wait("tensor", s_a, rr_done[gi - 1])

                    def mm_r(tt, t=t):
                        for k in range(DC):
                            for n in range(2):
                                mm = tt.matmul(ps[n][:, :],
                                               xrb[:, k * GT + t * 128:
                                                   k * GT + (t + 1) * 128],
                                               wa[:, k * 1024 + n * 512:
                                                  k * 1024 + (n + 1) * 512],
                                               start=(k == 0),
                                               stop=(k == DC - 1))
                                if k == DC - 1 and n == 1:
                                    s_pe.inc(mm)
                    P.emit("tensor", mm_r)
                    pe_r_val[gi] = s_pe.n
                    P.wait("scalar", s_pe, pe_r_val[gi])
                    if gi >= 2:
                        P.wait("scalar", s_v, dve_ffn_val[gi - 2])
                    P.emit("scalar", lambda s, gi=gi: (
                        s.activation(rr[gi % 2][:, 0:512], ps[0][:, :], AF.Sigmoid,
                                     bias=0.0, scale=1.0),
                        s_a.inc(s.activation(rr[gi % 2][:, 512:1024], ps[1][:, :],
                                             AF.Sigmoid, bias=0.0, scale=1.0)))[1])
                    rr_done[gi] = s_a.n
                    P.wait("tensor", s_v, kact_done[(g, DC - 1)])
                    if gi == 0:
                        P.wait("tensor", s_wc, wv_ready)
                    if gi >= 1:
                        P.wait("tensor", s_v, dve_ffn_val[gi - 1])

                    def mm_v(tt, t=t):
                        for k in range(DC):
                            for n in range(2):
                                mm = tt.matmul(ps[2 + n][:, :],
                                               kact[:, k * GT + t * 128:
                                                    k * GT + (t + 1) * 128],
                                               wb[:, k * 1024 + n * 512:
                                                  k * 1024 + (n + 1) * 512],
                                               start=(k == 0),
                                               stop=(k == DC - 1))
                                if k == DC - 1 and n == 1:
                                    s_pe.inc(mm)
                    P.emit("tensor", mm_v)
                    pe_v_val[gi] = s_pe.n
                    P.wait("vector", s_pe, pe_v_val[gi])
                    P.wait("vector", s_a, rr_done[gi])
                    if gi >= 8:
                        P.wait("vector", s_a, acc_done[gi - 8])
                    fdst = ffsl[gl][:, t * D:(t + 1) * D]
                    P.emit("vector", lambda v, fdst=fdst, gi=gi: v.tensor_mul(
                        fdst[:, 0:512], ps[2][:, :], rr[gi % 2][:, 0:512]))
                    P.emit("vector", lambda v, fdst=fdst, gi=gi: s_v.inc(
                        v.tensor_mul(fdst[:, 512:1024], ps[3][:, :],
                                     rr[gi % 2][:, 512:1024])))
                    dve_ffn_val[gi] = s_v.n
                    P.wait("scalar", s_v, dve_ffn_val[gi])
                    P.emit("scalar", lambda s, fdst=fdst, jl=jl: s.activation(
                        accsc, fdst, AF.Identity, bias=0.0, scale=1.0,
                        accum_out=agg[:, jl:jl + 1]))
                    P.emit("scalar", lambda s, fdst=fdst, jl=jl: s_a.inc(
                        s.activation(accsc, fdst, AF.Square, bias=0.0,
                                     scale=1.0,
                                     accum_out=agg[:, 8 + jl:9 + jl])))
                    acc_done[gi] = s_a.n

            if p >= 1:
                P.wait("scalar", s_v, p2d_outv_last)
            P.emit("scalar", lambda s: s_a.inc(s.activation(
                st6f_mu, agg[:, 0:8], AF.Identity, bias=0.0,
                scale=1.0 / 1024.0)))
            mu_done = s_a.n
            P.wait("vector", s_a, mu_done)
            P.emit("vector", lambda v: v.tensor_mul(
                rph[:, 16:24], st6f_mu, st6f_mu))
            P.emit("vector", lambda v: v.tensor_copy(rph[:, 8:16], st6f_mu))
            P.emit("vector", lambda v: s_v.inc(v.scalar_tensor_tensor(
                st6f_var, agg[:, 8:16], 1.0 / 1024.0, rph[:, 16:24],
                op0=ALU.mult, op1=ALU.subtract)))
            var_done = s_v.n
            P.wait("scalar", s_v, var_done)
            P.emit("scalar", lambda s: s.activation(
                rph[:, 16:24], st6f_var, AF.Ln, bias=eps_ap, scale=1.0))
            P.emit("scalar", lambda s: s_a.inc(s.activation(
                rph[:, 0:8], rph[:, 16:24], AF.Exp, bias=0.0, scale=-0.5)))
            rphi_done = s_a.n
            P.wait("vector", s_a, rphi_done)
            P.emit("vector", lambda v: s_v.inc(v.tensor_copy(
                rph[:, 8:16], rph[:, 0:8])))
            nmphi_done = s_v.n

            for j in range(8):
                gi = 8 * p + j
                gl = j // 4
                fsrc = ffsl[gl][:, (j % 4) * D:(j % 4 + 1) * D]
                if gi >= 2:
                    P.wait("sync", s_finp[gi % 2], out_stored[gi - 2])
                P.emit("sync", lambda q, gi=gi: s_xrp[gi % 2].inc(q.dma_start(
                    out=xt[gi % 2], in_=out_h[gi * 128:(gi + 1) * 128, :]), 16))
                xm_reread[gi] = s_xrp[gi % 2].n
                P.wait("vector", s_v, nmphi_done)
                P.wait("vector", s_xrp[gi % 2], xm_reread[gi])
                if gi >= 2:
                    P.wait("vector", s_finp[gi % 2], out_stored[gi - 2])
                P.emit("vector", lambda v, fsrc=fsrc, j=j, gi=gi: v.tensor_scalar(
                    xmr[gi % 2], fsrc, st6f_mu[:, j:j + 1], rph[:, j:j + 1],
                    op0=ALU.subtract, op1=ALU.mult))
                P.emit("vector", lambda v, gi=gi: s_v.inc(v.tensor_add(
                    att[gi % 2], xmr[gi % 2], xt[gi % 2])))
                outv_done = s_v.n
                P.wait("sync", s_v, outv_done)
                P.emit("sync", lambda q, gi=gi: s_finp[gi % 2].inc(q.dma_start(
                    out=out_h[gi * 128:(gi + 1) * 128, :], in_=att[gi % 2]), 16))
                out_stored[gi] = s_finp[gi % 2].n
                if j == 7:
                    p2d_outv_last = outv_done

        P.wait("sync", s_finp[0], s_finp[0].n)
        P.wait("sync", s_finp[1], s_finp[1].n)
        P.wait("sync", s_xmp[0], s_xmp[0].n)
        P.wait("sync", s_xmp[1], s_xmp[1].n)
        P.wait("sync", s_tap, s_tap.n)

    except _StopBuild:
        pass
    blk = ctx.enter_context(nc.Block())
    for eng_name, dec in (("sync", blk.sync), ("gpsimd", blk.gpsimd),
                          ("tensor", blk.tensor), ("vector", blk.vector),
                          ("scalar", blk.scalar)):
        def body(q, eng_name=eng_name):
            for fn in P.stream[eng_name]:
                fn(q)
        dec(body)
    ctx.close()
    return nc


def _host_inputs(inputs):
    x = np.ascontiguousarray(np.asarray(inputs["L0_attn_x"], dtype=np.float32))
    in_w = np.asarray(inputs["L0_attn_in_w"], np.float32)
    dt_w = np.asarray(inputs["L0_attn_dt_w"], np.float32)
    b_w = np.asarray(inputs["L0_attn_b_w"], np.float32)
    c_w = np.asarray(inputs["L0_attn_c_w"], np.float32)
    out_w = np.asarray(inputs["L0_attn_out_w"], np.float32)
    out_b = np.asarray(inputs["L0_attn_out_b"], np.float32)
    wr = np.asarray(inputs["L0_ffn_W_r"], np.float32)
    wk = np.asarray(inputs["L0_ffn_W_k"], np.float32)
    wv = np.asarray(inputs["L0_ffn_W_v"], np.float32)

    in_wT = np.ascontiguousarray(in_w.T)
    dtbc_wT = np.ascontiguousarray(np.concatenate([dt_w, b_w, c_w], 0).T)
    out_wT = np.ascontiguousarray(
        np.concatenate([out_w, out_b[:, None]], 1).T)
    wr_T = np.ascontiguousarray(wr.T)
    wk_T = np.ascontiguousarray(wk.T)
    wv_T = np.ascontiguousarray(wv.T)

    def fm(vec):
        return np.ascontiguousarray(
            np.asarray(vec, np.float32).reshape(-1, 128).T)

    conv_w = np.asarray(inputs["L0_attn_conv_w"], np.float32)[:, 0, :]
    vbase = np.zeros((128, 98), np.float32)
    vbase[:, 0:8] = fm(inputs["L0_attn_norm_w"])
    vbase[:, 8:16] = fm(inputs["L0_attn_norm_b"])
    vbase[:, 16:32] = fm(inputs["L0_attn_in_b"])
    vbase[:, 32:40] = fm(conv_w[:, 0])
    vbase[:, 40:48] = fm(conv_w[:, 1])
    vbase[:, 48:56] = fm(conv_w[:, 2])
    vbase[:, 56:64] = fm(inputs["L0_attn_conv_b"])
    vbase[:, 64:72] = fm(inputs["L0_attn_d"])
    vbase[:, 72:80] = fm(inputs["L0_ffn_mu_r"])
    vbase[:, 80:88] = fm(inputs["L0_ffn_mu_k"])

    v64 = np.zeros((S, 8), np.float32)
    v64[:, 0] = np.asarray(inputs["L0_attn_dt_b"], np.float32)
    v64[:, 1] = np.asarray(inputs["L0_attn_b_b"], np.float32)
    v64[:, 2] = np.asarray(inputs["L0_attn_c_b"], np.float32)
    v64[:, 3] = np.asarray(inputs["L0_attn_a_log"], np.float32)

    in_maps = []
    for c in range(8):
        b, h = c // 2, c % 2
        vfm = vbase.copy()
        vfm[:, 88 + (c ^ 1)] = 1.0
        vfm[:, 96] = 1.0 if h == 1 else 0.0
        vfm[:, 97] = 1.0 if h == 0 else 0.0
        in_maps.append({
            "x": np.ascontiguousarray(x[b, h * T:(h + 1) * T, :]),
            "in_wT": in_wT, "dtbc_wT": dtbc_wT, "out_wT": out_wT,
            "wr_T": wr_T, "wk_T": wk_T, "wv_T": wv_T,
            "vfm": vfm, "v64": v64,
        })
    return in_maps


def kernel(**inputs):
    nc = build_nc()
    in_maps = _host_inputs(inputs)
    res = run_bass_kernel_spmd(nc, in_maps, core_ids=list(range(8)))
    out = np.empty((4, 8192, 1024), np.float32)
    for c in range(8):
        b, h = c // 2, c % 2
        out[b, h * T:(h + 1) * T, :] = res.results[c]["out"]
    return out


# version 03 (final)
# speedup vs baseline: 1.0870x; 1.0870x over previous
import contextlib
import numpy as np
import concourse.bass as bass
import concourse.mybir as mybir
from concourse.bass_utils import run_bass_kernel_spmd

F32 = mybir.dt.float32
BF16 = mybir.dt.bfloat16
AF = mybir.ActivationFunctionType
ALU = mybir.AluOpType

T = 4096
D = 1024
DC = 8
GT = 512
NG = 8
NTIL = 32
S = 64
DB = 16
CW = T + 32
EPS = 1e-5

_DEBUG_TAPS = []
_STOP_AFTER = None
_P1A_GROUPS = NG if False else 8
_SKIP_DTBC = False
_SKIP_EVAC = False
_P1A_M = 16

TAP_SPECS = {
    "xproj": ([128, DC * CW], BF16), "xconv": ([128, DC * CW], BF16),
    "zdbg": ([128, DC * GT], BF16), "decay": ([S, T], BF16),
    "bhat": ([S, T], BF16), "chat": ([S, T], BF16), "states": ([S, T], BF16),
    "y": ([S, T], BF16), "vfm_dbg": ([128, DC * CW], BF16),
    "xn0": ([128, DC * GT], BF16), "ibx": ([128, 16], F32),
    "r1s": ([128, NTIL], F32), "gath1": ([128, 8 * 17], F32),
    "kact0": ([128, DC * GT], BF16), "xr0": ([128, DC * GT], BF16),
    "xk0": ([128, DC * GT], BF16), "lnst0": ([128, 16], F32),
    "vsc": ([T, D], BF16), "att0": ([128, D], F32), "xm0": ([128, D], F32),
}


_DRY = [False]


class _DryInstr:
    def then_inc(self, *a, **k):
        return self

    def branch_hint(self, *a, **k):
        return self


class _DryQ:
    def __getattr__(self, name):
        return lambda *a, **k: _DryInstr()


class Sem:
    def __init__(self, ctx, nc, name):
        self.h = ctx.enter_context(nc.semaphore(name))
        self.n = 0

    def inc(self, instr, k=1):
        if _DRY[0]:
            self.n += k
            return instr
        instr.then_inc(self.h, k)
        return instr


class Prog:
    ENGS = ("sync", "gpsimd", "tensor", "vector", "scalar")

    def __init__(self):
        self.stream = {e: [] for e in self.ENGS}

    def emit(self, eng, fn):
        _DRY[0] = True
        try:
            fn(_DryQ())
        finally:
            _DRY[0] = False
        self.stream[eng].append(fn)

    def wait(self, eng, sem, val):
        if val > 0:
            self.emit(eng, lambda q, s=sem.h, v=val: q.wait_ge(s, v))


def build_nc():
    nc = bass.Bass("TRN2", target_bir_lowering=False,
                   detect_race_conditions=False)
    ctx = contextlib.ExitStack()

    x_h = nc.dram_tensor("x", [T, D], F32, kind="ExternalInput").ap()
    in_wT = nc.dram_tensor("in_wT", [D, 2 * D], F32, kind="ExternalInput").ap()
    dtbc_wT = nc.dram_tensor("dtbc_wT", [D, 192], F32, kind="ExternalInput").ap()
    out_wT = nc.dram_tensor("out_wT", [D + S + 1, D], F32, kind="ExternalInput").ap()
    wr_h = nc.dram_tensor("wr_T", [D, D], F32, kind="ExternalInput").ap()
    wk_h = nc.dram_tensor("wk_T", [D, D], F32, kind="ExternalInput").ap()
    wv_h = nc.dram_tensor("wv_T", [D, D], F32, kind="ExternalInput").ap()
    vfm_h = nc.dram_tensor("vfm", [128, 98], F32, kind="ExternalInput").ap()
    v64_h = nc.dram_tensor("v64", [S, 8], F32, kind="ExternalInput").ap()
    out_h = nc.dram_tensor("out", [T, D], F32, kind="ExternalOutput").ap()

    xn_sc = nc.dram_tensor("xn_sc", [T, D], BF16, kind="Internal").ap()
    hb_sc = nc.dram_tensor("hb_sc", [2048], F32, kind="Internal").ap()
    v_sc = nc.dram_tensor("v_sc", [T, D], BF16, kind="Internal").ap()
    cc1_in = nc.dram_tensor("cc1_in", [128, 17], F32, kind="Internal").ap()
    cc1_out = nc.dram_tensor("cc1_out", [8, 128, 17], F32, kind="Internal").ap()
    cc2_in = nc.dram_tensor("cc2_in", [128, 8], F32, kind="Internal").ap()
    cc2_out = nc.dram_tensor("cc2_out", [8, 128, 8], F32, kind="Internal").ap()

    taps = {}
    for name in _DEBUG_TAPS:
        shape, dt = TAP_SPECS[name]
        taps[name] = nc.dram_tensor("tap_" + name, shape, dt,
                                    kind="ExternalOutput").ap()

    bfsz, off = {}, 0

    def reg(name, size):
        nonlocal off
        bfsz[name] = (off, size)
        off += size

    reg("XBIG", DC * CW)
    reg("WA", 16384)
    reg("WB", 9216)
    reg("WC", 8 * 192)
    reg("STG", 4096)
    reg("XNFM", 2 * DC * GT)
    reg("D1", 4096)
    reg("BSCR", 8192)
    reg("YEXT", T)
    reg("CBF", 16)
    NBF = off
    bfa = ctx.enter_context(nc.sbuf_tensor("bfa", [128, NBF], BF16))

    def B(name, p0=0, p1=128):
        o, s = bfsz[name]
        return bfa[p0:p1, o:o + s]

    f32sz, off = {}, 0

    def reg2(name, size):
        nonlocal off
        f32sz[name] = (off, size)
        off += size

    reg2("XT", 2 * D)
    reg2("UT", D)
    reg2("AT", 2 * D)
    reg2("XM", 2 * D)
    reg2("SCR", 1024)
    reg2("ST6", 12)
    reg2("AGG", 28)
    reg2("R1S", NTIL)
    reg2("M1S", NTIL)
    reg2("STF", 16)
    reg2("RPH", 24)
    reg2("BND", 35)
    reg2("GTH", 8 * 17)
    reg2("ACC", 8 * 17)
    reg2("CST", 1)
    reg2("VFM", 98)
    reg2("IBX", 16)
    reg2("OMU", 16)
    reg2("A64", 2)
    NF32 = off
    f32a = ctx.enter_context(nc.sbuf_tensor("f32a", [128, NF32], F32))

    def F(name, p0=0, p1=128):
        o, s = f32sz[name]
        return f32a[p0:p1, o:o + s]

    v64 = ctx.enter_context(nc.sbuf_tensor("v64s", [S, 8], F32))
    ps = [ctx.enter_context(nc.psum_tensor("ps%d" % i, [128, 512], F32))
          for i in range(8)]

    s_ldc = Sem(ctx, nc, "s_ldc")
    s_xp = [Sem(ctx, nc, "s_x0"), Sem(ctx, nc, "s_x1")]
    s_wgp = [Sem(ctx, nc, "s_wg0"), Sem(ctx, nc, "s_wg1")]
    s_wc = Sem(ctx, nc, "s_wc")
    s_xnhp = [Sem(ctx, nc, "s_xnh0"), Sem(ctx, nc, "s_xnh1")]
    s_xnf = Sem(ctx, nc, "s_xnf")
    s_vhp = [Sem(ctx, nc, "s_vh0"), Sem(ctx, nc, "s_vh1")]
    s_vf = Sem(ctx, nc, "s_vf")
    s_cc = Sem(ctx, nc, "s_cc")
    s_xmp = [Sem(ctx, nc, "s_xm0"), Sem(ctx, nc, "s_xm1")]
    s_xrp = [Sem(ctx, nc, "s_xr0"), Sem(ctx, nc, "s_xr1")]
    s_tap = Sem(ctx, nc, "s_tap")
    s_pe = Sem(ctx, nc, "s_pe")
    s_ev = Sem(ctx, nc, "s_ev")
    s_a = Sem(ctx, nc, "s_a")
    s_v = Sem(ctx, nc, "s_v")
    s_g = Sem(ctx, nc, "s_g")
    s_finp = [Sem(ctx, nc, "s_fin0"), Sem(ctx, nc, "s_fin1")]

    P = Prog()

    class _StopBuild(Exception):
        pass

    def _maybe_stop(tag):
        if _STOP_AFTER == tag:
            for sm in (s_a, s_v, s_pe, s_g, s_wgp[0], s_wgp[1], s_wc,
                       s_cc, s_tap, s_xnf, s_vf, s_xnhp[0], s_xnhp[1],
                       s_vhp[0], s_vhp[1], s_xmp[0], s_xmp[1],
                       s_finp[0], s_finp[1], s_xp[0], s_xp[1],
                       s_xrp[0], s_xrp[1]):
                P.wait("sync", sm, sm.n)
            raise _StopBuild()

    try:

        xbig3 = B("XBIG").rearrange("p (c w) -> p c w", c=DC)
        wa = B("WA")
        wb = B("WB")
        wc = B("WC")
        stg = B("STG")
        xnfm = B("XNFM")
        d1 = B("D1")
        bscr = B("BSCR")
        yext_all = B("YEXT", 0, 65)
        vfm = F("VFM")
        ibx = F("IBX")
        eps_ap = F("CST")[:, 0:1]

        VC = dict(norm_w=0, norm_b=8, in_b=16, cw0=32, cw1=40, cw2=48, cb=56,
                  dvec=64, mur=72, muk=80, sel=88, gates=96)

        def vcol(name, k=0):
            return vfm[:, VC[name] + k:VC[name] + k + 1]

        pool = B("WB")
        decay = pool[0:64, 0:T]
        bhat = pool[0:64, T:2 * T]
        rawb = bscr[0:64, 0:T]
        rawc = bscr[0:64, T:2 * T]
        chat = bscr[0:64, 0:T]
        states = bscr[0:64, T:2 * T]

        scr1 = F("SCR", 0, 64)[:, 0:512]
        scr2 = F("SCR", 0, 64)[:, 512:1024]
        a_ap = F("A64", 0, 64)[:, 0:1]
        a4_ap = F("A64", 0, 64)[:, 1:2]

        xt = [F("XT")[:, i * D:(i + 1) * D] for i in range(2)]
        ut = F("UT")
        att = [F("AT")[:, i * D:(i + 1) * D] for i in range(2)]
        xmr = [F("XM")[:, i * D:(i + 1) * D] for i in range(2)]
        st6 = F("ST6")
        agg = F("AGG")
        r1s = F("R1S")
        m1s = F("M1S")
        xn_tm = [d1[:, i * D:(i + 1) * D] for i in range(2)]
        xnf = [xnfm[:, i * DC * GT:(i + 1) * DC * GT] for i in range(2)]
        zb = stg

        mu_ap = agg[:, 0:1]
        var_ap = agg[:, 1:2]
        lnv_ap = agg[:, 2:3]
        r_ap = agg[:, 3:4]
        nm_ap = agg[:, 4:5]
        t4a = agg[:, 8:12]
        t4b = agg[:, 12:16]
        t1a = agg[:, 16:17]
        t1b = agg[:, 17:18]

        def ln_stats_tile(v, x_ap):
            v.bn_stats(st6[:, 0:6], x_ap[:, 0:512])
            return v.bn_stats(st6[:, 6:12], x_ap[:, 512:1024])

        def ln_agg_tile(v, mu_out, var_out):
            m = [st6[:, i:i + 1] for i in (1, 4, 7, 10)]
            cv = [st6[:, i:i + 1] for i in (2, 5, 8, 11)]
            Tn = [agg[:, 8 + i:9 + i] for i in range(17)]
            v.tensor_mul(Tn[4], m[0], m[0])
            v.tensor_mul(Tn[5], m[1], m[1])
            v.tensor_mul(Tn[6], m[2], m[2])
            v.tensor_mul(Tn[7], m[3], m[3])
            v.tensor_add(Tn[0], m[0], m[1])
            v.tensor_add(Tn[1], m[2], m[3])
            v.tensor_add(Tn[2], cv[0], cv[1])
            v.tensor_add(Tn[3], cv[2], cv[3])
            v.tensor_add(Tn[8], Tn[4], Tn[5])
            v.tensor_add(Tn[9], Tn[6], Tn[7])
            v.tensor_add(Tn[10], Tn[0], Tn[1])
            v.tensor_add(Tn[11], Tn[2], Tn[3])
            v.tensor_add(Tn[12], Tn[8], Tn[9])
            v.tensor_scalar_mul(Tn[13], Tn[11], 1.0 / 1024.0)
            v.tensor_scalar_mul(mu_out, Tn[10], 0.25)
            v.scalar_tensor_tensor(Tn[15], Tn[12], 0.25, Tn[13],
                                   op0=ALU.mult, op1=ALU.add)
            v.tensor_mul(Tn[14], mu_out, mu_out)
            v.tensor_copy(Tn[16], Tn[8])
            return v.tensor_sub(var_out, Tn[15], Tn[14])

        def rsqrt_chain(s, var_in, r_out):
            s.activation(lnv_ap, var_in, AF.Ln, bias=eps_ap, scale=1.0)
            return s.activation(r_out, lnv_ap, AF.Exp, bias=0.0, scale=-0.5)

        def tap_dump(name, ap, wait_sems):
            if name in taps:
                for sm, v in wait_sems:
                    P.wait("sync", sm, v)
                P.emit("sync", lambda q, name=name, ap=ap: s_tap.inc(
                    q.dma_start(out=taps[name], in_=ap), 16))

        P.emit("sync", lambda q: s_ldc.inc(q.dma_start(out=vfm, in_=vfm_h), 16))
        P.emit("sync", lambda q: s_ldc.inc(q.dma_start(out=v64[:], in_=v64_h), 16))
        ldc_done = s_ldc.n

        P.emit("vector", lambda v: v.memset(eps_ap, EPS))
        P.emit("vector", lambda v: v.memset(yext_all[64:65, :], 1.0))
        def halo_zero(v):
            for c in range(DC):
                v.memset(xbig3[:, c, DB - 1:DB], 0.0)
                li = v.memset(xbig3[:, c, DB + T:DB + T + 1], 0.0)
            return li
        P.emit("vector", lambda v: s_v.inc(halo_zero(v)))
        halos_zeroed = s_v.n

        P.wait("vector", s_ldc, ldc_done)
        P.emit("vector", lambda v: s_v.inc(v.tensor_copy(
            B("CBF")[:, 0:8], vfm[:, VC["norm_b"]:VC["norm_b"] + 8])))
        nb_bf_done = s_v.n
        P.emit("vector", lambda v: v.tensor_scalar(
            F("OMU")[:, 0:8], vfm[:, VC["mur"]:VC["mur"] + 8], -1.0, 1.0,
            op0=ALU.mult, op1=ALU.add))
        P.emit("vector", lambda v: s_v.inc(v.tensor_scalar(
            F("OMU")[:, 8:16], vfm[:, VC["muk"]:VC["muk"] + 8], -1.0, 1.0,
            op0=ALU.mult, op1=ALU.add)))
        omu_done = s_v.n

        def omur(c):
            return F("OMU")[:, c:c + 1]

        def omuk(c):
            return F("OMU")[:, 8 + c:8 + c + 1]

        P.wait("scalar", s_ldc, ldc_done)
        P.emit("scalar", lambda s: s_a.inc(s.activation(
            scr1[:, 0:1], v64[:, 3:4], AF.Exp, bias=0.0, scale=1.0)))
        a_exp_done = s_a.n
        P.wait("vector", s_a, a_exp_done)
        P.emit("vector", lambda v: v.tensor_scalar_mul(a_ap, scr1[:, 0:1], -1.0))
        P.emit("vector", lambda v: s_v.inc(
            v.tensor_scalar_mul(a4_ap, scr1[:, 0:1], -1e-4)))
        a_done = s_v.n

        stage = [stg[:, i * 2048:(i + 1) * 2048] for i in range(2)]
        hb_vals, sc_vals = [], []
        for k in range(DC):
            par = k % 2
            if k >= 2:
                P.wait("gpsimd", s_pe, hb_vals[k - 2])
                P.wait("gpsimd", s_v, sc_vals[k - 2])
            P.emit("gpsimd", lambda g, par=par, k=k: s_wgp[par].inc(
                g.dma_start(out=stage[par], in_=in_wT[k * 128:(k + 1) * 128, :]), 16))
            ld = s_wgp[par].n
            P.wait("tensor", s_wgp[par], ld)
            if k == 0:
                P.wait("tensor", s_ldc, ldc_done)
                P.wait("tensor", s_v, nb_bf_done)

            def hb_mms(t, k=k, par=par):
                for n in range(4):
                    mm = t.matmul(ps[4 + n][0:1, :],
                                  B("CBF")[:, k:k + 1],
                                  stage[par][:, n * 512:(n + 1) * 512],
                                  start=(k == 0), stop=(k == DC - 1))
                    if n == 3:
                        s_pe.inc(mm)
            P.emit("tensor", hb_mms)
            hb_vals.append(s_pe.n)
            P.wait("vector", s_wgp[par], ld)
            P.emit("vector", lambda v, k=k, par=par: s_v.inc(v.tensor_scalar_mul(
                wa[:, k * 2048:(k + 1) * 2048], stage[par], vcol("norm_w", k))))
            sc_vals.append(s_v.n)
        wa_ready = s_v.n
        hb_done = s_pe.n
        hbrow = F("SCR")[0:1, 0:1024]
        P.wait("vector", s_pe, hb_done)
        for half in range(2):
            if half == 1:
                P.wait("vector", s_ldc, s_ldc.n)
            P.emit("vector", lambda v, half=half: v.tensor_copy(
                hbrow[:, 0:512], ps[4 + 2 * half][0:1, :]))
            P.emit("vector", lambda v, half=half: s_v.inc(v.tensor_copy(
                hbrow[:, 512:1024], ps[5 + 2 * half][0:1, :])))
            ev_done = s_v.n
            P.wait("sync", s_v, ev_done)
            P.emit("sync", lambda q, half=half: s_ldc.inc(q.dma_start(
                out=hb_sc[half * 1024:(half + 1) * 1024], in_=hbrow), 16))
        hb_in_hbm = s_ldc.n
        P.wait("sync", s_ldc, hb_in_hbm)
        P.emit("sync", lambda q: s_ldc.inc(q.dma_start(
            out=F("GTH")[:, 0:16],
            in_=hb_sc.rearrange("(m p) -> p m", p=128),
            allow_slow_non_contiguous=True), 16))
        hb_back = s_ldc.n
        P.wait("vector", s_ldc, hb_back)
        P.emit("vector", lambda v: s_v.inc(v.tensor_add(
            ibx, F("GTH")[:, 0:16], vfm[:, VC["in_b"]:VC["in_b"] + 16])))
        ibx_ready = s_v.n

        P.emit("gpsimd", lambda g: s_wc.inc(g.dma_start(
            out=wc.rearrange("p (c d) -> p c d", c=DC),
            in_=dtbc_wT.rearrange("(c p) d -> p c d", p=128)), 16))
        wc_ready = s_wc.n

        tap_dump("ibx", ibx, [(s_v, ibx_ready)])
        _maybe_stop("setup")

        pA = [ps[0], ps[1]]
        pdt, pb_, pc_ = ps[2], ps[3], ps[4]

        xn_done_tile = [0] * NTIL
        xnh_stored = [0] * NTIL
        x_wait = [0] * NTIL
        pe_mm_val = {}
        pe_dtbc_val = [0] * NG
        dt_chain_done = [0] * NG
        xnf_done_g = [0] * NG
        ev_base = s_ev.n
        ng_run = min(NG, _P1A_GROUPS)

        def emit_xload(gi, gate_sem=None, gate_val=0):
            if gate_sem is not None and gate_val > 0:
                P.wait("sync", gate_sem, gate_val)
            P.emit("sync", lambda q, gi=gi: s_xp[gi % 2].inc(q.dma_start(
                out=xt[gi % 2], in_=x_h[gi * 128:(gi + 1) * 128, :]), 16))
            x_wait[gi] = s_xp[gi % 2].n

        emit_xload(0)
        emit_xload(1)

        for g in range(ng_run):
            gs, ge = g * GT, (g + 1) * GT
            for t in range(4):
                gi = 4 * g + t
                P.wait("vector", s_xp[gi % 2], x_wait[gi])
                P.emit("vector", lambda v, gi=gi: ln_stats_tile(v, xt[gi % 2]))
                P.emit("vector", lambda v: s_v.inc(ln_agg_tile(v, mu_ap, var_ap)))
                ln1_done = s_v.n
                if gi == 0 and "lnst0" in taps:
                    def snap0(v):
                        v.tensor_copy(F("STF")[:, 0:12], st6[:, 0:12])
                        v.tensor_copy(F("STF")[:, 12:13], mu_ap)
                        return v.tensor_copy(F("STF")[:, 13:14], var_ap)
                    P.emit("vector", lambda v: s_v.inc(snap0(v)))
                    snap_done = s_v.n
                    P.wait("sync", s_v, snap_done)
                    P.emit("sync", lambda q: s_tap.inc(q.dma_start(
                        out=taps["lnst0"], in_=F("STF")[:, 0:16]), 16))
                P.wait("scalar", s_v, ln1_done)
                P.emit("scalar", lambda s, gi=gi: s_a.inc(
                    rsqrt_chain(s, var_ap, r1s[:, gi:gi + 1])))
                r1_done = s_a.n
                P.wait("vector", s_a, r1_done)
                P.emit("vector", lambda v, gi=gi: s_v.inc(v.tensor_scalar(
                    m1s[:, gi:gi + 1], mu_ap, r1s[:, gi:gi + 1], -1.0,
                    op0=ALU.mult, op1=ALU.mult)))
                nm1_done = s_v.n
                P.wait("scalar", s_v, nm1_done)
                P.emit("scalar", lambda s, gi=gi: s_a.inc(s.activation(
                    ut, xt[gi % 2], AF.Identity,
                    bias=m1s[:, gi:gi + 1], scale=r1s[:, gi:gi + 1])))
                u_done = s_a.n
                P.wait("vector", s_a, u_done)
                P.emit("vector", lambda v: ln_stats_tile(v, ut))
                P.emit("vector", lambda v: s_v.inc(ln_agg_tile(v, mu_ap, var_ap)))
                ln2_done = s_v.n
                P.wait("scalar", s_v, ln2_done)
                P.emit("scalar", lambda s: s_a.inc(
                    rsqrt_chain(s, var_ap, r_ap)))
                r2_done = s_a.n
                P.wait("vector", s_a, r2_done)
                P.emit("vector", lambda v: s_v.inc(v.tensor_scalar(
                    nm_ap, mu_ap, r_ap, -1.0, op0=ALU.mult, op1=ALU.mult)))
                nm2_done = s_v.n
                P.wait("scalar", s_v, nm2_done)
                if gi >= 2:
                    P.wait("scalar", s_xnhp[gi % 2], xnh_stored[gi - 2])
                P.emit("scalar", lambda s, gi=gi: s_a.inc(s.activation(
                    xn_tm[gi % 2], ut, AF.Identity,
                    bias=nm_ap, scale=r_ap)))
                xn_done_tile[gi] = s_a.n
                P.wait("sync", s_a, xn_done_tile[gi])
                P.emit("sync", lambda q, gi=gi: s_xnhp[gi % 2].inc(q.dma_start(
                    out=xn_sc[gi * 128:(gi + 1) * 128, :], in_=xn_tm[gi % 2]), 16))
                xnh_stored[gi] = s_xnhp[gi % 2].n
                if gi + 2 < NTIL:
                    emit_xload(gi + 2)

            _maybe_stop("p1a_ln%d" % g)
            if g >= 2:
                P.wait("sync", s_pe, pe_dtbc_val[g - 2])
            P.wait("sync", s_xnhp[0], s_xnhp[0].n)
            P.wait("sync", s_xnhp[1], s_xnhp[1].n)
            for c in range(DC):
                P.emit("sync", lambda q, g=g, c=c, gs=gs, ge=ge: s_xnf.inc(
                    q.dma_start_transpose(
                        xnf[g % 2][:, c * GT:(c + 1) * GT],
                        xn_sc[gs:ge, c * 128:(c + 1) * 128]), 16))
            xnf_done_g[g] = s_xnf.n

            if g == 0:
                tap_dump("xn0", xnf[0], [(s_xnf, xnf_done_g[0])])

            _maybe_stop("p1a_tr%d" % g)
            P.wait("tensor", s_xnf, xnf_done_g[g])
            if g == 0:
                P.wait("tensor", s_v, wa_ready)
                P.wait("tensor", s_wc, wc_ready)
            for m in range(_P1A_M):
                need = ev_base + _P1A_M * g + m - 1
                if need > ev_base and not _SKIP_EVAC:
                    P.wait("tensor", s_ev, need)

                def mm_in(tt, g=g, m=m):
                    for k in range(DC):
                        mm = tt.matmul(pA[m % 2][:, :],
                                       wa[:, k * 2048 + m * 128:
                                          k * 2048 + (m + 1) * 128],
                                       xnf[g % 2][:, k * GT:(k + 1) * GT],
                                       start=(k == 0), stop=(k == DC - 1))
                        if k == DC - 1:
                            s_pe.inc(mm)
                P.emit("tensor", mm_in)
                pe_mm_val[(g, m)] = s_pe.n
            P.wait("tensor", s_ev, ev_base + _P1A_M * g + _P1A_M)
            if g >= 1 and not _SKIP_DTBC:
                P.wait("tensor", s_a, dt_chain_done[g - 1])

            def mm_dtbc(tt, g=g):
                for bi, pp in enumerate((pdt, pb_, pc_)):
                    for k in range(DC):
                        mm = tt.matmul(pp[0:64, :],
                                       wc[:, k * 192 + bi * 64:
                                          k * 192 + (bi + 1) * 64],
                                       zb[:, k * GT:(k + 1) * GT],
                                       start=(k == 0), stop=(k == DC - 1))
                        if bi == 2 and k == DC - 1:
                            s_pe.inc(mm)
            P.emit("tensor", mm_dtbc)
            pe_dtbc_val[g] = s_pe.n

            _maybe_stop("p1a_mm%d" % g)
            if _SKIP_EVAC:
                P.wait("scalar", s_pe, pe_mm_val[(g, 15)])
                dt_chain_done[g] = s_a.n
                continue
            if g == 0:
                P.wait("scalar", s_v, ibx_ready)
            for m in range(_P1A_M):
                P.wait("scalar", s_pe, pe_mm_val[(g, m)])
                if m < 8:
                    dst = xbig3[:, m, DB + gs:DB + ge]
                else:
                    dst = zb[:, (m - 8) * GT:(m - 7) * GT]
                P.emit("scalar", lambda s, m=m, dst=dst: s_ev.inc(s.activation(
                    dst, pA[m % 2][:, :], AF.Identity,
                    bias=ibx[:, m:m + 1], scale=1.0)))
            if _SKIP_DTBC:
                dt_chain_done[g] = s_a.n
                continue
            P.wait("scalar", s_pe, pe_dtbc_val[g])
            if g == 0:
                P.wait("scalar", s_v, a_done)
            P.emit("scalar", lambda s: s.activation(
                scr1, pdt[0:64, :], AF.Exp, bias=v64[:, 0:1], scale=1.0))
            P.emit("scalar", lambda s: s.activation(
                scr2, scr1, AF.Ln, bias=1.0, scale=1.0))
            P.emit("scalar", lambda s, gs=gs, ge=ge: s.activation(
                decay[:, gs:ge], scr2, AF.Exp, bias=a4_ap, scale=a_ap))
            P.emit("scalar", lambda s, gs=gs, ge=ge: s.activation(
                rawb[:, gs:ge], pb_[0:64, :], AF.Identity,
                bias=v64[:, 1:2], scale=1.0))
            P.emit("scalar", lambda s, gs=gs, ge=ge: s_a.inc(s.activation(
                rawc[:, gs:ge], pc_[0:64, :], AF.Identity,
                bias=v64[:, 2:3], scale=1.0)))
            dt_chain_done[g] = s_a.n

        p1a_act_done = s_a.n
        _maybe_stop("p1a")
        tap_dump("xproj", B("XBIG"), [(s_a, p1a_act_done)])
        tap_dump("zdbg", zb, [(s_a, p1a_act_done)])
        tap_dump("r1s", r1s, [(s_a, p1a_act_done)])

        P.emit("scalar", lambda s: s.activation(bhat, rawb, AF.Tanh,
                                                bias=0.0, scale=1.0))
        P.emit("scalar", lambda s: s_a.inc(s.activation(
            chat, rawc, AF.Tanh, bias=0.0, scale=1.0)))
        tanh_bc_done = s_a.n

        P.wait("vector", s_a, tanh_bc_done)
        P.emit("vector", lambda v: s_v.inc(v.tensor_tensor_scan(
            states, decay, bhat, 0.0, op0=ALU.mult, op1=ALU.add)))
        scan1_done = s_v.n

        bnd = F("BND")[:, 0:17]
        P.emit("vector", lambda v: v.memset(bnd, 0.0))
        def bnd_fill(v):
            for c in range(DC):
                v.tensor_copy(bnd[:, c:c + 1], xbig3[:, c, DB:DB + 1])
                v.tensor_copy(bnd[:, 8 + c:9 + c],
                              xbig3[:, c, DB + T - 1:DB + T])
        P.emit("vector", bnd_fill)
        P.emit("vector", lambda v: s_v.inc(v.tensor_copy(
            bnd[0:64, 16:17], states[:, T - 1:T])))
        bnd_done = s_v.n
        P.wait("sync", s_v, bnd_done)
        P.emit("sync", lambda q: s_cc.inc(q.dma_start(out=cc1_in, in_=bnd), 16))
        cc1_staged = s_cc.n
        P.wait("gpsimd", s_cc, cc1_staged)
        P.emit("gpsimd", lambda g: s_g.inc(g.collective_compute(
            "AllGather", ALU.bypass, replica_groups=[list(range(8))],
            ins=[cc1_in], outs=[cc1_out])))
        cc1_done = s_g.n
        P.wait("sync", s_g, cc1_done)
        P.emit("sync", lambda q: s_cc.inc(q.dma_start(
            out=F("GTH").rearrange("p (r k) -> p r k", r=8),
            in_=cc1_out.rearrange("r p k -> p r k")), 16))
        gth1_done = s_cc.n
        tap_dump("gath1", F("GTH"), [(s_cc, gth1_done)])

        cw0 = [vcol("cw0", c) for c in range(DC)]
        cw1 = [vcol("cw1", c) for c in range(DC)]
        cw2 = [vcol("cw2", c) for c in range(DC)]
        cpre = [xnfm[:, i * DC * GT:(i + 1) * DC * GT] for i in range(2)]
        sv0 = F("BND")[:, 18:26]
        sv1 = F("BND")[:, 26:34]
        taps_done = [0] * NG
        tanh_done = [0] * NG
        for g in range(NG):
            gs, ge = g * GT, (g + 1) * GT
            if g == 0:
                P.wait("vector", s_a, p1a_act_done)
                P.wait("vector", s_v, halos_zeroed)
            if g >= 2:
                P.wait("vector", s_a, tanh_done[g - 2])

            def conv_taps(v, g=g, gs=gs, ge=ge):
                buf = cpre[g % 2]
                for c in range(DC):
                    b = buf[:, c * GT:(c + 1) * GT]
                    v.tensor_scalar_mul(b, xbig3[:, c, DB - 1 + gs:DB - 1 + ge],
                                        cw0[c])
                    v.scalar_tensor_tensor(b, xbig3[:, c, DB + gs:DB + ge],
                                           cw1[c], b, op0=ALU.mult, op1=ALU.add)
                    v.scalar_tensor_tensor(b, xbig3[:, c, DB + 1 + gs:DB + 1 + ge],
                                           cw2[c], b, op0=ALU.mult, op1=ALU.add)
            P.emit("vector", conv_taps)
            if g == 0:
                def sv0_fill(v):
                    for c in range(DC):
                        li = v.tensor_copy(sv0[:, c:c + 1],
                                           cpre[0][:, c * GT:c * GT + 1])
                    return li
                P.emit("vector", lambda v: s_v.inc(sv0_fill(v)))
            elif g == NG - 1:
                def sv1_fill(v):
                    for c in range(DC):
                        li = v.tensor_copy(
                            sv1[:, c:c + 1],
                            cpre[(NG - 1) % 2][:, c * GT + GT - 1:c * GT + GT])
                    return li
                P.emit("vector", lambda v: s_v.inc(sv1_fill(v)))
            else:
                P.emit("vector", lambda v, g=g: s_v.inc(v.tensor_copy(
                    agg[:, 26:27], cpre[g % 2][:, 0:1])))
            taps_done[g] = s_v.n
            if g >= 1:
                gp = g - 1
                P.wait("scalar", s_v, taps_done[g])

                def tanh_g(s, gp=gp):
                    for c in range(DC):
                        ai = s.activation(
                            xbig3[:, c, DB + gp * GT:DB + (gp + 1) * GT],
                            cpre[gp % 2][:, c * GT:(c + 1) * GT],
                            AF.Tanh, bias=vcol("cb", c), scale=1.0)
                        if c == DC - 1:
                            s_a.inc(ai)
                P.emit("scalar", tanh_g)
                tanh_done[gp] = s_a.n
        P.wait("scalar", s_v, taps_done[NG - 1])

        def tanh_last(s):
            gp = NG - 1
            for c in range(DC):
                ai = s.activation(xbig3[:, c, DB + gp * GT:DB + (gp + 1) * GT],
                                  cpre[gp % 2][:, c * GT:(c + 1) * GT],
                                  AF.Tanh, bias=vcol("cb", c), scale=1.0)
                if c == DC - 1:
                    s_a.inc(ai)
        P.emit("scalar", tanh_last)
        tanh_done[NG - 1] = s_a.n

        acc = [F("ACC")[:, 0:17], F("ACC")[:, 17:34]]
        sel = vfm[:, VC["sel"]:VC["sel"] + 8]
        g_left = vfm[:, VC["gates"]:VC["gates"] + 1]
        g_right = vfm[:, VC["gates"] + 1:VC["gates"] + 2]
        gth = F("GTH")
        P.wait("vector", s_cc, gth1_done)
        P.wait("vector", s_a, tanh_done[NG - 1])

        PR = [F("ACC")[:, i * 17:(i + 1) * 17] for i in range(8)]

        def select_tree(v, width):
            for r in range(8):
                v.tensor_scalar_mul(PR[r][:, 0:width],
                                    gth[:, r * width:(r + 1) * width],
                                    sel[:, r:r + 1])
            for r in range(4):
                v.tensor_add(PR[2 * r][:, 0:width], PR[2 * r][:, 0:width],
                             PR[2 * r + 1][:, 0:width])
            v.tensor_add(PR[0][:, 0:width], PR[0][:, 0:width],
                         PR[2][:, 0:width])
            v.tensor_add(PR[4][:, 0:width], PR[4][:, 0:width],
                         PR[6][:, 0:width])
            v.tensor_copy(PR[3][:, 0:width], PR[2][:, 0:width])
            v.tensor_add(PR[1][:, 0:width], PR[0][:, 0:width],
                         PR[4][:, 0:width])
            return v.tensor_copy(PR[2][:, 0:width], PR[0][:, 0:width])
        P.emit("vector", lambda v: select_tree(v, 17))
        recv = PR[1]
        def halo_fill(v):
            for c in range(DC):
                v.tensor_scalar_mul(xbig3[:, c, DB - 1:DB],
                                    recv[:, 8 + c:9 + c], g_left)
                v.tensor_scalar_mul(xbig3[:, c, DB + T:DB + T + 1],
                                    recv[:, c:c + 1], g_right)
        P.emit("vector", halo_fill)
        carry = F("BND")[0:64, 34:35]
        P.emit("vector", lambda v: s_v.inc(v.tensor_scalar_mul(
            carry, recv[0:64, 16:17], g_left[0:64, :])))
        ex1_done = s_v.n

        fix0 = F("GTH")[:, 0:8]
        fix1 = F("GTH")[:, 8:16]

        def fixup_pre(v):
            for c in range(DC):
                v.scalar_tensor_tensor(fix0[:, c:c + 1],
                                       xbig3[:, c, DB - 1:DB],
                                       cw0[c], sv0[:, c:c + 1],
                                       op0=ALU.mult, op1=ALU.add)
                li = v.scalar_tensor_tensor(fix1[:, c:c + 1],
                                            xbig3[:, c, DB + T:DB + T + 1],
                                            cw2[c], sv1[:, c:c + 1],
                                            op0=ALU.mult, op1=ALU.add)
            return li
        def fixup_pre_inc(v):
            s_v.inc(fixup_pre(v))
        P.emit("vector", fixup_pre_inc)
        fixpre_done = s_v.n
        P.wait("scalar", s_v, fixpre_done)

        def fixup_tanh(s):
            for c in range(DC):
                s.activation(xbig3[:, c, DB:DB + 1], fix0[:, c:c + 1], AF.Tanh,
                             bias=vcol("cb", c), scale=1.0)
                ai = s.activation(xbig3[:, c, DB + T - 1:DB + T],
                                  fix1[:, c:c + 1],
                                  AF.Tanh, bias=vcol("cb", c), scale=1.0)
                if c == DC - 1:
                    s_a.inc(ai)
        P.emit("scalar", fixup_tanh)
        conv_all_done = s_a.n

        P.wait("vector", s_v, ex1_done)
        P.emit("vector", lambda v: v.tensor_tensor_scan(
            states, decay, bhat, carry[:, 0:1], op0=ALU.mult, op1=ALU.add))
        yext = yext_all[0:64, :]
        P.emit("vector", lambda v: s_v.inc(v.tensor_mul(yext, chat, states)))
        y_done = s_v.n

        for nm, apv, sm, vv in (("decay", decay, s_a, tanh_bc_done),
                                ("bhat", bhat, s_a, tanh_bc_done),
                                ("chat", chat, s_a, tanh_bc_done),
                                ("states", states, s_v, y_done),
                                ("y", yext, s_v, y_done)):
            tap_dump(nm, apv, [(sm, vv)])
        tap_dump("xconv", B("XBIG"), [(s_a, conv_all_done), (s_v, y_done)])

        owt_stage = [bscr[:, i * 1024:(i + 1) * 1024] for i in range(2)]
        P.wait("gpsimd", s_v, y_done)
        P.wait("gpsimd", s_a, tanh_bc_done)
        owt_scaled = [0] * 9
        for k in range(9):
            par = k % 2
            if k >= 2:
                P.wait("gpsimd", s_v, owt_scaled[k - 2])
            if k < 8:
                P.emit("gpsimd", lambda g, k=k, par=par: s_wgp[par].inc(g.dma_start(
                    out=owt_stage[par], in_=out_wT[k * 128:(k + 1) * 128, :]), 16))
            else:
                P.emit("gpsimd", lambda g, par=par: s_wgp[par].inc(g.dma_start(
                    out=owt_stage[par][0:65, :], in_=out_wT[1024:1089, :]), 16))
            ldv = s_wgp[par].n
            P.wait("vector", s_wgp[par], ldv)
            if k < 8:
                P.emit("vector", lambda v, k=k, par=par: s_v.inc(v.tensor_scalar_mul(
                    wb[:, k * 1024:(k + 1) * 1024], owt_stage[par], vcol("dvec", k))))
            else:
                P.emit("vector", lambda v, par=par: s_v.inc(v.tensor_copy(
                    wb[0:65, 8 * 1024:9 * 1024], owt_stage[par][0:65, :])))
            owt_scaled[k] = s_v.n
        owt_ready = s_v.n
        _maybe_stop("p1c")

        vtm = [d1[:, i * D:(i + 1) * D] for i in range(2)]
        x2_wait = [0] * NTIL
        vtm_done = [0] * NTIL
        vtm_stored = [0] * NTIL
        psfree_val = [0] * NTIL
        lnAap_tile = [0] * NTIL
        xm_done_tile = [0] * NTIL
        xm_stored = [0] * NTIL
        pe2_val = [0] * NTIL
        pe2_group = [0] * NG
        vf_done_g = [0] * NG

        def emit_x2load(gi):
            P.emit("sync", lambda q, gi=gi: s_xp[gi % 2].inc(q.dma_start(
                out=xt[gi % 2], in_=x_h[gi * 128:(gi + 1) * 128, :]), 16))
            x2_wait[gi] = s_xp[gi % 2].n

        emit_x2load(0)
        emit_x2load(1)

        for g in range(NG):
            gs = g * GT
            for t in range(4):
                gi = 4 * g + t
                tw0 = gs + t * 128
                pn = [ps[2 * (gi % 2)], ps[2 * (gi % 2) + 1]]

                if gi == 0:
                    P.wait("tensor", s_v, owt_ready)
                    P.wait("tensor", s_a, conv_all_done)
                    P.wait("tensor", s_v, y_done)
                if gi >= 2:
                    P.wait("tensor", s_v, psfree_val[gi - 2])

                def mm_out(tt, tw0=tw0, pn=pn):
                    for k in range(9):
                        if k < 8:
                            lhs = xbig3[:, k, DB + tw0:DB + tw0 + 128]
                            r0, r1_ = 0, 128
                        else:
                            lhs = yext_all[0:65, tw0:tw0 + 128]
                            r0, r1_ = 0, 65
                        for n in range(2):
                            mm = tt.matmul(pn[n][:, :], lhs,
                                           wb[r0:r1_, k * 1024 + n * 512:
                                              k * 1024 + (n + 1) * 512],
                                           start=(k == 0), stop=(k == 8))
                            if k == 8 and n == 1:
                                s_pe.inc(mm)
                P.emit("tensor", mm_out)
                pe2_val[gi] = s_pe.n
                if t == 3:
                    pe2_group[g] = s_pe.n

                P.wait("scalar", s_xp[gi % 2], x2_wait[gi])
                if gi >= 1:
                    P.wait("scalar", s_v, xm_done_tile[gi - 1])
                P.emit("scalar", lambda s, gi=gi: s_a.inc(s.activation(
                    ut, xt[gi % 2], AF.Identity,
                    bias=m1s[:, gi:gi + 1], scale=r1s[:, gi:gi + 1])))
                u2_done = s_a.n
                P.wait("vector", s_pe, pe2_val[gi])
                P.wait("vector", s_a, u2_done)
                if gi >= 2:
                    P.wait("vector", s_a, lnAap_tile[gi - 2])
                P.emit("vector", lambda v, gi=gi, pn=pn: v.tensor_add(
                    att[gi % 2][:, 0:512], pn[0][:, :], ut[:, 0:512]))
                P.emit("vector", lambda v, gi=gi, pn=pn: s_v.inc(v.tensor_add(
                    att[gi % 2][:, 512:1024], pn[1][:, :], ut[:, 512:1024])))
                psfree_val[gi] = s_v.n
                if gi == 0 and "att0" in taps:
                    P.wait("sync", s_v, psfree_val[0])
                    P.emit("sync", lambda q: s_tap.inc(q.dma_start(
                        out=taps["att0"], in_=att[0]), 16))
                P.emit("vector", lambda v, gi=gi: ln_stats_tile(v, att[gi % 2]))
                P.emit("vector", lambda v: s_v.inc(ln_agg_tile(v, mu_ap, var_ap)))
                lnA_done = s_v.n
                P.wait("scalar", s_v, lnA_done)
                P.emit("scalar", lambda s: s_a.inc(
                    rsqrt_chain(s, var_ap, r_ap)))
                rA_done = s_a.n
                P.wait("vector", s_a, rA_done)
                P.emit("vector", lambda v: s_v.inc(v.tensor_scalar(
                    nm_ap, mu_ap, r_ap, -1.0, op0=ALU.mult, op1=ALU.mult)))
                nmA_done = s_v.n
                P.wait("scalar", s_v, nmA_done)
                P.emit("scalar", lambda s, gi=gi: s_a.inc(s.activation(
                    ut, att[gi % 2], AF.Identity,
                    bias=nm_ap, scale=r_ap)))
                lnAap_tile[gi] = s_a.n
                P.wait("vector", s_a, lnAap_tile[gi])
                if gi >= 2:
                    P.wait("vector", s_a, vtm_done[gi - 2])
                    P.wait("vector", s_xmp[gi % 2], xm_stored[gi - 2])
                P.emit("vector", lambda v, gi=gi: s_v.inc(v.tensor_add(
                    xmr[gi % 2], xt[gi % 2], ut)))
                xm_done_tile[gi] = s_v.n
                if gi == 0 and "xm0" in taps:
                    P.wait("sync", s_v, xm_done_tile[0])
                    P.emit("sync", lambda q: s_tap.inc(q.dma_start(
                        out=taps["xm0"], in_=xmr[0]), 16))
                P.wait("sync", s_v, xm_done_tile[gi])
                P.emit("sync", lambda q, gi=gi: s_xmp[gi % 2].inc(q.dma_start(
                    out=out_h[gi * 128:(gi + 1) * 128, :], in_=xmr[gi % 2]), 16))
                xm_stored[gi] = s_xmp[gi % 2].n
                P.emit("vector", lambda v, gi=gi: ln_stats_tile(v, xmr[gi % 2]))
                P.emit("vector", lambda v: s_v.inc(ln_agg_tile(v, mu_ap, var_ap)))
                ln3_done = s_v.n
                P.wait("scalar", s_v, ln3_done)
                P.emit("scalar", lambda s: s_a.inc(
                    rsqrt_chain(s, var_ap, r_ap)))
                r3_done = s_a.n
                P.wait("vector", s_a, r3_done)
                P.emit("vector", lambda v: s_v.inc(v.tensor_scalar(
                    nm_ap, mu_ap, r_ap, -1.0, op0=ALU.mult, op1=ALU.mult)))
                nm3_done = s_v.n
                P.wait("scalar", s_v, nm3_done)
                if gi >= 2:
                    P.wait("scalar", s_vhp[gi % 2], vtm_stored[gi - 2])
                P.emit("scalar", lambda s, gi=gi: s_a.inc(s.activation(
                    vtm[gi % 2], xmr[gi % 2], AF.Identity,
                    bias=nm_ap, scale=r_ap)))
                vtm_done[gi] = s_a.n
                P.wait("sync", s_a, vtm_done[gi])
                P.emit("sync", lambda q, gi=gi: s_vhp[gi % 2].inc(q.dma_start(
                    out=v_sc[gi * 128:(gi + 1) * 128, :], in_=vtm[gi % 2]), 16))
                vtm_stored[gi] = s_vhp[gi % 2].n
                if gi + 2 < NTIL:
                    emit_x2load(gi + 2)

            P.wait("sync", s_pe, pe2_group[g])
            P.wait("sync", s_vhp[0], s_vhp[0].n)
            P.wait("sync", s_vhp[1], s_vhp[1].n)
            for c in range(DC):
                P.emit("sync", lambda q, c=c, gs=gs: s_vf.inc(
                    q.dma_start_transpose(
                        B("XBIG")[:, c * CW + DB + gs:c * CW + DB + gs + GT],
                        v_sc[gs:gs + GT, c * 128:(c + 1) * 128]), 16))
            vf_done_g[g] = s_vf.n

        if "vsc" in taps:
            P.wait("sync", s_vhp[0], s_vhp[0].n)
            P.wait("sync", s_vhp[1], s_vhp[1].n)
            P.emit("sync", lambda q: s_tap.inc(
                q.dma_start(out=taps["vsc"], in_=v_sc), 16))
        P.wait("gpsimd", s_pe, pe2_group[0])
        P.emit("gpsimd", lambda g: s_wgp[0].inc(g.dma_start(
            out=wa[:, 0:8192].rearrange("p (c d) -> p c d", c=DC),
            in_=wr_h.rearrange("(c p) d -> p c d", p=128)), 16))
        P.emit("gpsimd", lambda g: s_wgp[1].inc(g.dma_start(
            out=wa[:, 8192:16384].rearrange("p (c d) -> p c d", c=DC),
            in_=wk_h.rearrange("(c p) d -> p c d", p=128)), 16))
        wrk_ready = (s_wgp[0].n, s_wgp[1].n)

        bnd2 = F("BND")[:, 0:8]
        P.wait("vector", s_vf, vf_done_g[NG - 1])
        def bnd2_fill(v):
            for c in range(DC):
                li = v.tensor_copy(bnd2[:, c:c + 1],
                                   xbig3[:, c, DB + T - 1:DB + T])
            return li
        P.emit("vector", lambda v: s_v.inc(bnd2_fill(v)))
        bnd2_done = s_v.n
        P.wait("sync", s_v, bnd2_done)
        P.emit("sync", lambda q: s_cc.inc(q.dma_start(out=cc2_in, in_=bnd2), 16))
        cc2_staged = s_cc.n
        P.wait("gpsimd", s_cc, cc2_staged)
        P.emit("gpsimd", lambda g: s_g.inc(g.collective_compute(
            "AllGather", ALU.bypass, replica_groups=[list(range(8))],
            ins=[cc2_in], outs=[cc2_out])))
        cc2_done = s_g.n
        P.wait("sync", s_g, cc2_done)
        P.emit("sync", lambda q: s_cc.inc(q.dma_start(
            out=F("GTH")[:, 0:64].rearrange("p (r k) -> p r k", r=8),
            in_=cc2_out.rearrange("r p k -> p r k")), 16))
        gth2_done = s_cc.n
        P.wait("vector", s_cc, gth2_done)

        P.emit("vector", lambda v: select_tree(v, 8))
        def vprev_fill(v):
            for c in range(DC):
                li = v.tensor_scalar_mul(xbig3[:, c, DB - 1:DB],
                                         recv[:, c:c + 1], g_left)
            return li
        P.emit("vector", lambda v: s_v.inc(vprev_fill(v)))
        ex2_done = s_v.n
        tap_dump("vfm_dbg", B("XBIG"), [(s_v, ex2_done)])
        _maybe_stop("p2a")

        P.wait("gpsimd", s_pe, pe2_group[NG - 1])
        P.emit("gpsimd", lambda g: s_wc.inc(g.dma_start(
            out=wb[:, 0:8192].rearrange("p (c d) -> p c d", c=DC),
            in_=wv_h.rearrange("(c p) d -> p c d", p=128)), 16))
        wv_ready = s_wc.n

        xrb = xnfm[:, 0:DC * GT]
        xkb = xnfm[:, DC * GT:2 * DC * GT]
        s1b = bscr[:, 0:512]
        krel = [bscr[:, 512 + i * 512:512 + (i + 1) * 512] for i in range(2)]
        rr = [bscr[:, 1536 + i * D:1536 + (i + 1) * D] for i in range(2)]
        accsc = bscr[:, 3584:4608]
        kact = B("YEXT")
        ffsl = [stg, d1]
        st6f_mu = F("STF")[:, 0:8]
        st6f_var = F("STF")[:, 8:16]
        rph = F("RPH")
        mur = [vcol("mur", c) for c in range(DC)]
        muk = [vcol("muk", c) for c in range(DC)]

        out_stored = [0] * NTIL
        xm_reread = [0] * NTIL
        pe_k_val = {}
        pe_r_val = {}
        pe_v_val = {}
        rr_done = {}
        krel_val = {}
        kact_done = {}
        dve_ffn_val = {}
        acc_done = {}
        p2d_outv_last = 0
        nmphi_of = {}

        def emit_p2d_read(gi):
            if gi >= 2:
                P.wait("sync", s_finp[gi % 2], out_stored[gi - 2])
            P.emit("sync", lambda q, gi=gi: s_xrp[gi % 2].inc(q.dma_start(
                out=xt[gi % 2], in_=out_h[gi * 128:(gi + 1) * 128, :]), 16))
            xm_reread[gi] = s_xrp[gi % 2].n

        def emit_p2d_tile(psrc, j):
            gi = 8 * psrc + j
            gl_ = j // 4
            fsrc = ffsl[gl_][:, (j % 4) * D:(j % 4 + 1) * D]
            P.wait("vector", s_v, nmphi_of[psrc])
            P.wait("vector", s_xrp[gi % 2], xm_reread[gi])
            if gi >= 2:
                P.wait("vector", s_finp[gi % 2], out_stored[gi - 2])
            P.emit("vector", lambda v, fsrc=fsrc, j=j, gi=gi: v.tensor_scalar(
                xmr[gi % 2], fsrc, st6f_mu[:, j:j + 1], rph[:, j:j + 1],
                op0=ALU.subtract, op1=ALU.mult))
            P.emit("vector", lambda v, gi=gi: s_v.inc(v.tensor_add(
                att[gi % 2], xmr[gi % 2], xt[gi % 2])))
            outv = s_v.n
            P.wait("sync", s_v, outv)
            P.emit("sync", lambda q, gi=gi: s_finp[gi % 2].inc(q.dma_start(
                out=out_h[gi * 128:(gi + 1) * 128, :], in_=att[gi % 2]), 16))
            out_stored[gi] = s_finp[gi % 2].n
            return outv

        for p in range(4):
            for gl, g in enumerate((2 * p, 2 * p + 1)):
                gs = g * GT
                if g == 0:
                    P.wait("vector", s_v, ex2_done)
                    P.wait("vector", s_v, omu_done)
                else:
                    P.wait("vector", s_vf, vf_done_g[g])
                    P.wait("vector", s_pe, pe_v_val[4 * (g - 1) + 3])

                def mk_xrk(v, g=g, gs=gs):
                    for c in range(DC):
                        vwin = xbig3[:, c, DB + gs:DB + gs + GT]
                        vprev = xbig3[:, c, DB - 1 + gs:DB - 1 + gs + GT]
                        v.tensor_scalar_mul(s1b, vwin, mur[c])
                        v.scalar_tensor_tensor(xrb[:, c * GT:(c + 1) * GT],
                                               vprev, omur(c), s1b,
                                               op0=ALU.mult, op1=ALU.add)
                        v.tensor_scalar_mul(s1b, vwin, muk[c])
                        li = v.scalar_tensor_tensor(xkb[:, c * GT:(c + 1) * GT],
                                                    vprev, omuk(c), s1b,
                                                    op0=ALU.mult, op1=ALU.add)
                        if c == DC - 1:
                            s_v.inc(li)
                P.emit("vector", mk_xrk)
                xrk_done = s_v.n
                if g == 0:
                    tap_dump("xr0", xrb, [(s_v, xrk_done)])
                    tap_dump("xk0", xkb, [(s_v, xrk_done)])

                P.wait("tensor", s_v, xrk_done)
                if g == 0:
                    P.wait("tensor", s_wgp[0], wrk_ready[0])
                    P.wait("tensor", s_wgp[1], wrk_ready[1])
                for m in range(DC):
                    prev = (g, m - 2) if m >= 2 else (
                        (g - 1, DC - 2 + m) if g >= 1 else None)
                    if prev is not None and prev in krel_val:
                        P.wait("tensor", s_a, krel_val[prev])

                    def mm_k(tt, m=m):
                        pp = ps[4 + m % 2]
                        for k in range(DC):
                            mm = tt.matmul(pp[:, :],
                                           wa[:, 8192 + k * 1024 + m * 128:
                                              8192 + k * 1024 + (m + 1) * 128],
                                           xkb[:, k * GT:(k + 1) * GT],
                                           start=(k == 0), stop=(k == DC - 1))
                            if k == DC - 1:
                                s_pe.inc(mm)
                    P.emit("tensor", mm_k)
                    pe_k_val[(g, m)] = s_pe.n
                    P.wait("scalar", s_pe, pe_k_val[(g, m)])
                    if prev is not None and prev in kact_done:
                        P.wait("scalar", s_v, kact_done[prev])
                    P.emit("scalar", lambda s, m=m: s_a.inc(s.activation(
                        krel[m % 2], ps[4 + m % 2][:, :], AF.Relu,
                        bias=0.0, scale=1.0)))
                    krel_val[(g, m)] = s_a.n
                    P.wait("vector", s_a, krel_val[(g, m)])
                    if m == 0 and g >= 1:
                        P.wait("vector", s_pe, pe_v_val[4 * (g - 1) + 3])
                    P.emit("vector", lambda v, m=m: s_v.inc(v.tensor_mul(
                        kact[:, m * GT:(m + 1) * GT], krel[m % 2], krel[m % 2])))
                    kact_done[(g, m)] = s_v.n
                if g == 0:
                    tap_dump("kact0", kact, [(s_v, kact_done[(0, DC - 1)])])

                for t in range(4):
                    gi = 4 * g + t
                    jl = 4 * gl + t
                    if p >= 1:
                        pj = 8 * (p - 1) + jl
                        if jl == 0:
                            emit_p2d_read(pj)
                        if jl + 1 < 8:
                            emit_p2d_read(pj + 1)
                        ov = emit_p2d_tile(p - 1, jl)
                        if jl == 7:
                            p2d_outv_last = ov
                    if gi >= 1:
                        P.wait("tensor", s_a, rr_done[gi - 1])

                    def mm_r(tt, t=t):
                        for k in range(DC):
                            for n in range(2):
                                mm = tt.matmul(ps[n][:, :],
                                               xrb[:, k * GT + t * 128:
                                                   k * GT + (t + 1) * 128],
                                               wa[:, k * 1024 + n * 512:
                                                  k * 1024 + (n + 1) * 512],
                                               start=(k == 0),
                                               stop=(k == DC - 1))
                                if k == DC - 1 and n == 1:
                                    s_pe.inc(mm)
                    P.emit("tensor", mm_r)
                    pe_r_val[gi] = s_pe.n
                    P.wait("scalar", s_pe, pe_r_val[gi])
                    if gi >= 2:
                        P.wait("scalar", s_v, dve_ffn_val[gi - 2])
                    P.emit("scalar", lambda s, gi=gi: (
                        s.activation(rr[gi % 2][:, 0:512], ps[0][:, :], AF.Sigmoid,
                                     bias=0.0, scale=1.0),
                        s_a.inc(s.activation(rr[gi % 2][:, 512:1024], ps[1][:, :],
                                             AF.Sigmoid, bias=0.0, scale=1.0)))[1])
                    rr_done[gi] = s_a.n
                    P.wait("tensor", s_v, kact_done[(g, DC - 1)])
                    if gi == 0:
                        P.wait("tensor", s_wc, wv_ready)
                    if gi >= 1:
                        P.wait("tensor", s_v, dve_ffn_val[gi - 1])

                    def mm_v(tt, t=t):
                        for k in range(DC):
                            for n in range(2):
                                mm = tt.matmul(ps[2 + n][:, :],
                                               kact[:, k * GT + t * 128:
                                                    k * GT + (t + 1) * 128],
                                               wb[:, k * 1024 + n * 512:
                                                  k * 1024 + (n + 1) * 512],
                                               start=(k == 0),
                                               stop=(k == DC - 1))
                                if k == DC - 1 and n == 1:
                                    s_pe.inc(mm)
                    P.emit("tensor", mm_v)
                    pe_v_val[gi] = s_pe.n
                    P.wait("vector", s_pe, pe_v_val[gi])
                    P.wait("vector", s_a, rr_done[gi])
                    if gi >= 8:
                        P.wait("vector", s_a, acc_done[gi - 8])
                    fdst = ffsl[gl][:, t * D:(t + 1) * D]
                    P.emit("vector", lambda v, fdst=fdst, gi=gi: v.tensor_mul(
                        fdst[:, 0:512], ps[2][:, :], rr[gi % 2][:, 0:512]))
                    P.emit("vector", lambda v, fdst=fdst, gi=gi: s_v.inc(
                        v.tensor_mul(fdst[:, 512:1024], ps[3][:, :],
                                     rr[gi % 2][:, 512:1024])))
                    dve_ffn_val[gi] = s_v.n
                    P.wait("scalar", s_v, dve_ffn_val[gi])
                    P.emit("scalar", lambda s, fdst=fdst, jl=jl: s.activation(
                        accsc, fdst, AF.Identity, bias=0.0, scale=1.0,
                        accum_out=agg[:, jl:jl + 1]))
                    P.emit("scalar", lambda s, fdst=fdst, jl=jl: s_a.inc(
                        s.activation(accsc, fdst, AF.Square, bias=0.0,
                                     scale=1.0,
                                     accum_out=agg[:, 8 + jl:9 + jl])))
                    acc_done[gi] = s_a.n

            if p >= 1:
                P.wait("scalar", s_v, p2d_outv_last)
            P.emit("scalar", lambda s: s_a.inc(s.activation(
                st6f_mu, agg[:, 0:8], AF.Identity, bias=0.0,
                scale=1.0 / 1024.0)))
            mu_done = s_a.n
            P.wait("vector", s_a, mu_done)
            P.emit("vector", lambda v: v.tensor_mul(
                rph[:, 16:24], st6f_mu, st6f_mu))
            P.emit("vector", lambda v: v.tensor_copy(rph[:, 8:16], st6f_mu))
            P.emit("vector", lambda v: s_v.inc(v.scalar_tensor_tensor(
                st6f_var, agg[:, 8:16], 1.0 / 1024.0, rph[:, 16:24],
                op0=ALU.mult, op1=ALU.subtract)))
            var_done = s_v.n
            P.wait("scalar", s_v, var_done)
            P.emit("scalar", lambda s: s.activation(
                rph[:, 16:24], st6f_var, AF.Ln, bias=eps_ap, scale=1.0))
            P.emit("scalar", lambda s: s_a.inc(s.activation(
                rph[:, 0:8], rph[:, 16:24], AF.Exp, bias=0.0, scale=-0.5)))
            rphi_done = s_a.n
            P.wait("vector", s_a, rphi_done)
            P.emit("vector", lambda v: s_v.inc(v.tensor_copy(
                rph[:, 8:16], rph[:, 0:8])))
            nmphi_of[p] = s_v.n

        for j in range(8):
            pj = 24 + j
            if j == 0:
                emit_p2d_read(pj)
            if j + 1 < 8:
                emit_p2d_read(pj + 1)
            emit_p2d_tile(3, j)

        P.wait("sync", s_finp[0], s_finp[0].n)
        P.wait("sync", s_finp[1], s_finp[1].n)
        P.wait("sync", s_xmp[0], s_xmp[0].n)
        P.wait("sync", s_xmp[1], s_xmp[1].n)
        P.wait("sync", s_tap, s_tap.n)

    except _StopBuild:
        pass
    blk = ctx.enter_context(nc.Block())
    for eng_name, dec in (("sync", blk.sync), ("gpsimd", blk.gpsimd),
                          ("tensor", blk.tensor), ("vector", blk.vector),
                          ("scalar", blk.scalar)):
        def body(q, eng_name=eng_name):
            for fn in P.stream[eng_name]:
                fn(q)
        dec(body)
    ctx.close()
    return nc


def _host_inputs(inputs):
    x = np.ascontiguousarray(np.asarray(inputs["L0_attn_x"], dtype=np.float32))
    in_w = np.asarray(inputs["L0_attn_in_w"], np.float32)
    dt_w = np.asarray(inputs["L0_attn_dt_w"], np.float32)
    b_w = np.asarray(inputs["L0_attn_b_w"], np.float32)
    c_w = np.asarray(inputs["L0_attn_c_w"], np.float32)
    out_w = np.asarray(inputs["L0_attn_out_w"], np.float32)
    out_b = np.asarray(inputs["L0_attn_out_b"], np.float32)
    wr = np.asarray(inputs["L0_ffn_W_r"], np.float32)
    wk = np.asarray(inputs["L0_ffn_W_k"], np.float32)
    wv = np.asarray(inputs["L0_ffn_W_v"], np.float32)

    in_wT = np.ascontiguousarray(in_w.T)
    dtbc_wT = np.ascontiguousarray(np.concatenate([dt_w, b_w, c_w], 0).T)
    out_wT = np.ascontiguousarray(
        np.concatenate([out_w, out_b[:, None]], 1).T)
    wr_T = np.ascontiguousarray(wr.T)
    wk_T = np.ascontiguousarray(wk.T)
    wv_T = np.ascontiguousarray(wv.T)

    def fm(vec):
        return np.ascontiguousarray(
            np.asarray(vec, np.float32).reshape(-1, 128).T)

    conv_w = np.asarray(inputs["L0_attn_conv_w"], np.float32)[:, 0, :]
    vbase = np.zeros((128, 98), np.float32)
    vbase[:, 0:8] = fm(inputs["L0_attn_norm_w"])
    vbase[:, 8:16] = fm(inputs["L0_attn_norm_b"])
    vbase[:, 16:32] = fm(inputs["L0_attn_in_b"])
    vbase[:, 32:40] = fm(conv_w[:, 0])
    vbase[:, 40:48] = fm(conv_w[:, 1])
    vbase[:, 48:56] = fm(conv_w[:, 2])
    vbase[:, 56:64] = fm(inputs["L0_attn_conv_b"])
    vbase[:, 64:72] = fm(inputs["L0_attn_d"])
    vbase[:, 72:80] = fm(inputs["L0_ffn_mu_r"])
    vbase[:, 80:88] = fm(inputs["L0_ffn_mu_k"])

    v64 = np.zeros((S, 8), np.float32)
    v64[:, 0] = np.asarray(inputs["L0_attn_dt_b"], np.float32)
    v64[:, 1] = np.asarray(inputs["L0_attn_b_b"], np.float32)
    v64[:, 2] = np.asarray(inputs["L0_attn_c_b"], np.float32)
    v64[:, 3] = np.asarray(inputs["L0_attn_a_log"], np.float32)

    in_maps = []
    for c in range(8):
        b, h = c // 2, c % 2
        vfm = vbase.copy()
        vfm[:, 88 + (c ^ 1)] = 1.0
        vfm[:, 96] = 1.0 if h == 1 else 0.0
        vfm[:, 97] = 1.0 if h == 0 else 0.0
        in_maps.append({
            "x": np.ascontiguousarray(x[b, h * T:(h + 1) * T, :]),
            "in_wT": in_wT, "dtbc_wT": dtbc_wT, "out_wT": out_wT,
            "wr_T": wr_T, "wk_T": wk_T, "wv_T": wv_T,
            "vfm": vfm, "v64": v64,
        })
    return in_maps


def kernel(**inputs):
    nc = build_nc()
    in_maps = _host_inputs(inputs)
    res = run_bass_kernel_spmd(nc, in_maps, core_ids=list(range(8)))
    out = np.empty((4, 8192, 1024), np.float32)
    for c in range(8):
        b, h = c // 2, c % 2
        out[b, h * T:(h + 1) * T, :] = res.results[c]["out"]
    return out


# version 04
# speedup vs baseline: 1.0872x; 1.0002x over previous
import contextlib
import numpy as np
import concourse.bass as bass
import concourse.mybir as mybir
from concourse.bass_utils import run_bass_kernel_spmd

F32 = mybir.dt.float32
BF16 = mybir.dt.bfloat16
AF = mybir.ActivationFunctionType
ALU = mybir.AluOpType

T = 4096
D = 1024
DC = 8
GT = 512
NG = 8
NTIL = 32
S = 64
DB = 16
CW = T + 32
EPS = 1e-5

_DEBUG_TAPS = []
_STOP_AFTER = None
_P1A_GROUPS = NG if False else 8
_SKIP_DTBC = False
_SKIP_EVAC = False
_P1A_M = 16

TAP_SPECS = {
    "xproj": ([128, DC * CW], BF16), "xconv": ([128, DC * CW], BF16),
    "zdbg": ([128, DC * GT], BF16), "decay": ([S, T], BF16),
    "bhat": ([S, T], BF16), "chat": ([S, T], BF16), "states": ([S, T], BF16),
    "y": ([S, T], BF16), "vfm_dbg": ([128, DC * CW], BF16),
    "xn0": ([128, DC * GT], BF16), "ibx": ([128, 16], F32),
    "r1s": ([128, NTIL], F32), "gath1": ([128, 8 * 17], F32),
    "kact0": ([128, DC * GT], BF16), "xr0": ([128, DC * GT], BF16),
    "xk0": ([128, DC * GT], BF16), "lnst0": ([128, 16], F32),
    "vsc": ([T, D], BF16), "att0": ([128, D], F32), "xm0": ([128, D], F32),
}


_DRY = [False]


class _DryInstr:
    def then_inc(self, *a, **k):
        return self

    def branch_hint(self, *a, **k):
        return self


class _DryQ:
    def __getattr__(self, name):
        return lambda *a, **k: _DryInstr()


class Sem:
    def __init__(self, ctx, nc, name):
        self.h = ctx.enter_context(nc.semaphore(name))
        self.n = 0

    def inc(self, instr, k=1):
        if _DRY[0]:
            self.n += k
            return instr
        instr.then_inc(self.h, k)
        return instr


class Prog:
    ENGS = ("sync", "gpsimd", "tensor", "vector", "scalar")

    def __init__(self):
        self.stream = {e: [] for e in self.ENGS}

    def emit(self, eng, fn):
        _DRY[0] = True
        try:
            fn(_DryQ())
        finally:
            _DRY[0] = False
        self.stream[eng].append(fn)

    def wait(self, eng, sem, val):
        if val > 0:
            self.emit(eng, lambda q, s=sem.h, v=val: q.wait_ge(s, v))


def build_nc():
    nc = bass.Bass("TRN2", target_bir_lowering=False,
                   detect_race_conditions=False)
    ctx = contextlib.ExitStack()

    x_h = nc.dram_tensor("x", [T, D], F32, kind="ExternalInput").ap()
    in_wT = nc.dram_tensor("in_wT", [D, 2 * D], F32, kind="ExternalInput").ap()
    dtbc_wT = nc.dram_tensor("dtbc_wT", [D, 192], F32, kind="ExternalInput").ap()
    out_wT = nc.dram_tensor("out_wT", [D + S + 1, D], F32, kind="ExternalInput").ap()
    wr_h = nc.dram_tensor("wr_T", [D, D], F32, kind="ExternalInput").ap()
    wk_h = nc.dram_tensor("wk_T", [D, D], F32, kind="ExternalInput").ap()
    wv_h = nc.dram_tensor("wv_T", [D, D], F32, kind="ExternalInput").ap()
    vfm_h = nc.dram_tensor("vfm", [128, 98], F32, kind="ExternalInput").ap()
    v64_h = nc.dram_tensor("v64", [S, 8], F32, kind="ExternalInput").ap()
    out_h = nc.dram_tensor("out", [T, D], F32, kind="ExternalOutput").ap()

    xn_sc = nc.dram_tensor("xn_sc", [T, D], BF16, kind="Internal").ap()
    hb_sc = nc.dram_tensor("hb_sc", [2048], F32, kind="Internal").ap()
    v_sc = nc.dram_tensor("v_sc", [T, D], BF16, kind="Internal").ap()
    cc1_in = nc.dram_tensor("cc1_in", [128, 17], F32, kind="Internal").ap()
    cc1_out = nc.dram_tensor("cc1_out", [8, 128, 17], F32, kind="Internal").ap()
    cc2_in = nc.dram_tensor("cc2_in", [128, 8], F32, kind="Internal").ap()
    cc2_out = nc.dram_tensor("cc2_out", [8, 128, 8], F32, kind="Internal").ap()

    taps = {}
    for name in _DEBUG_TAPS:
        shape, dt = TAP_SPECS[name]
        taps[name] = nc.dram_tensor("tap_" + name, shape, dt,
                                    kind="ExternalOutput").ap()

    bfsz, off = {}, 0

    def reg(name, size):
        nonlocal off
        bfsz[name] = (off, size)
        off += size

    reg("XBIG", DC * CW)
    reg("WA", 16384)
    reg("WB", 9216)
    reg("WC", 8 * 192)
    reg("STG", 4096)
    reg("XNFM", 2 * DC * GT)
    reg("D1", 4096)
    reg("BSCR", 8192)
    reg("YEXT", T)
    reg("CBF", 16)
    NBF = off
    bfa = ctx.enter_context(nc.sbuf_tensor("bfa", [128, NBF], BF16))

    def B(name, p0=0, p1=128):
        o, s = bfsz[name]
        return bfa[p0:p1, o:o + s]

    f32sz, off = {}, 0

    def reg2(name, size):
        nonlocal off
        f32sz[name] = (off, size)
        off += size

    reg2("XT", 2 * D)
    reg2("UT", D)
    reg2("AT", 2 * D)
    reg2("XM", 2 * D)
    reg2("SCR", 1024)
    reg2("ST6", 12)
    reg2("AGG", 28)
    reg2("R1S", NTIL)
    reg2("M1S", NTIL)
    reg2("STF", 16)
    reg2("RPH", 24)
    reg2("BND", 35)
    reg2("GTH", 8 * 17)
    reg2("ACC", 8 * 17)
    reg2("CST", 1)
    reg2("VFM", 98)
    reg2("IBX", 16)
    reg2("OMU", 16)
    reg2("A64", 2)
    NF32 = off
    f32a = ctx.enter_context(nc.sbuf_tensor("f32a", [128, NF32], F32))

    def F(name, p0=0, p1=128):
        o, s = f32sz[name]
        return f32a[p0:p1, o:o + s]

    v64 = ctx.enter_context(nc.sbuf_tensor("v64s", [S, 8], F32))
    ps = [ctx.enter_context(nc.psum_tensor("ps%d" % i, [128, 512], F32))
          for i in range(8)]

    s_ldc = Sem(ctx, nc, "s_ldc")
    s_xp = [Sem(ctx, nc, "s_x0"), Sem(ctx, nc, "s_x1")]
    s_wgp = [Sem(ctx, nc, "s_wg0"), Sem(ctx, nc, "s_wg1")]
    s_wc = Sem(ctx, nc, "s_wc")
    s_xnhp = [Sem(ctx, nc, "s_xnh0"), Sem(ctx, nc, "s_xnh1")]
    s_xnf = Sem(ctx, nc, "s_xnf")
    s_vhp = [Sem(ctx, nc, "s_vh0"), Sem(ctx, nc, "s_vh1")]
    s_vf = Sem(ctx, nc, "s_vf")
    s_cc = Sem(ctx, nc, "s_cc")
    s_xmp = [Sem(ctx, nc, "s_xm0"), Sem(ctx, nc, "s_xm1")]
    s_xrp = [Sem(ctx, nc, "s_xr0"), Sem(ctx, nc, "s_xr1")]
    s_tap = Sem(ctx, nc, "s_tap")
    s_pe = Sem(ctx, nc, "s_pe")
    s_ev = Sem(ctx, nc, "s_ev")
    s_a = Sem(ctx, nc, "s_a")
    s_v = Sem(ctx, nc, "s_v")
    s_g = Sem(ctx, nc, "s_g")
    s_finp = [Sem(ctx, nc, "s_fin0"), Sem(ctx, nc, "s_fin1")]

    P = Prog()

    class _StopBuild(Exception):
        pass

    def _maybe_stop(tag):
        if _STOP_AFTER == tag:
            for sm in (s_a, s_v, s_pe, s_g, s_wgp[0], s_wgp[1], s_wc,
                       s_cc, s_tap, s_xnf, s_vf, s_xnhp[0], s_xnhp[1],
                       s_vhp[0], s_vhp[1], s_xmp[0], s_xmp[1],
                       s_finp[0], s_finp[1], s_xp[0], s_xp[1],
                       s_xrp[0], s_xrp[1]):
                P.wait("sync", sm, sm.n)
            raise _StopBuild()

    try:

        xbig3 = B("XBIG").rearrange("p (c w) -> p c w", c=DC)
        wa = B("WA")
        wb = B("WB")
        wc = B("WC")
        stg = B("STG")
        xnfm = B("XNFM")
        d1 = B("D1")
        bscr = B("BSCR")
        yext_all = B("YEXT", 0, 65)
        vfm = F("VFM")
        ibx = F("IBX")
        eps_ap = F("CST")[:, 0:1]

        VC = dict(norm_w=0, norm_b=8, in_b=16, cw0=32, cw1=40, cw2=48, cb=56,
                  dvec=64, mur=72, muk=80, sel=88, gates=96)

        def vcol(name, k=0):
            return vfm[:, VC[name] + k:VC[name] + k + 1]

        pool = B("WB")
        decay = pool[0:64, 0:T]
        bhat = pool[0:64, T:2 * T]
        rawb = bscr[0:64, 0:T]
        rawc = bscr[0:64, T:2 * T]
        chat = bscr[0:64, 0:T]
        states = bscr[0:64, T:2 * T]

        scr1 = F("SCR", 0, 64)[:, 0:512]
        scr2 = F("SCR", 0, 64)[:, 512:1024]
        a_ap = F("A64", 0, 64)[:, 0:1]
        a4_ap = F("A64", 0, 64)[:, 1:2]

        xt = [F("XT")[:, i * D:(i + 1) * D] for i in range(2)]
        ut = F("UT")
        att = [F("AT")[:, i * D:(i + 1) * D] for i in range(2)]
        xmr = [F("XM")[:, i * D:(i + 1) * D] for i in range(2)]
        st6 = F("ST6")
        agg = F("AGG")
        r1s = F("R1S")
        m1s = F("M1S")
        xn_tm = [d1[:, i * D:(i + 1) * D] for i in range(2)]
        xnf = [xnfm[:, i * DC * GT:(i + 1) * DC * GT] for i in range(2)]
        zb = stg

        mu_ap = agg[:, 0:1]
        var_ap = agg[:, 1:2]
        lnv_ap = agg[:, 2:3]
        r_ap = agg[:, 3:4]
        nm_ap = agg[:, 4:5]
        t4a = agg[:, 8:12]
        t4b = agg[:, 12:16]
        t1a = agg[:, 16:17]
        t1b = agg[:, 17:18]

        def ln_stats_tile(v, x_ap):
            v.bn_stats(st6[:, 0:6], x_ap[:, 0:512])
            return v.bn_stats(st6[:, 6:12], x_ap[:, 512:1024])

        def ln_agg_tile(v, mu_out, var_out):
            m = [st6[:, i:i + 1] for i in (1, 4, 7, 10)]
            cv = [st6[:, i:i + 1] for i in (2, 5, 8, 11)]
            Tn = [agg[:, 8 + i:9 + i] for i in range(17)]
            v.tensor_mul(Tn[4], m[0], m[0])
            v.tensor_mul(Tn[5], m[1], m[1])
            v.tensor_mul(Tn[6], m[2], m[2])
            v.tensor_mul(Tn[7], m[3], m[3])
            v.tensor_add(Tn[0], m[0], m[1])
            v.tensor_add(Tn[1], m[2], m[3])
            v.tensor_add(Tn[2], cv[0], cv[1])
            v.tensor_add(Tn[3], cv[2], cv[3])
            v.tensor_add(Tn[8], Tn[4], Tn[5])
            v.tensor_add(Tn[9], Tn[6], Tn[7])
            v.tensor_add(Tn[10], Tn[0], Tn[1])
            v.tensor_add(Tn[11], Tn[2], Tn[3])
            v.tensor_add(Tn[12], Tn[8], Tn[9])
            v.tensor_scalar_mul(Tn[13], Tn[11], 1.0 / 1024.0)
            v.tensor_scalar_mul(mu_out, Tn[10], 0.25)
            v.scalar_tensor_tensor(Tn[15], Tn[12], 0.25, Tn[13],
                                   op0=ALU.mult, op1=ALU.add)
            v.tensor_mul(Tn[14], mu_out, mu_out)
            v.tensor_copy(Tn[16], Tn[8])
            return v.tensor_sub(var_out, Tn[15], Tn[14])

        def rsqrt_chain(s, var_in, r_out):
            s.activation(lnv_ap, var_in, AF.Ln, bias=eps_ap, scale=1.0)
            return s.activation(r_out, lnv_ap, AF.Exp, bias=0.0, scale=-0.5)

        def tap_dump(name, ap, wait_sems):
            if name in taps:
                for sm, v in wait_sems:
                    P.wait("sync", sm, v)
                P.emit("sync", lambda q, name=name, ap=ap: s_tap.inc(
                    q.dma_start(out=taps[name], in_=ap), 16))

        P.emit("sync", lambda q: s_ldc.inc(q.dma_start(out=vfm, in_=vfm_h), 16))
        P.emit("sync", lambda q: s_ldc.inc(q.dma_start(out=v64[:], in_=v64_h), 16))
        ldc_done = s_ldc.n

        P.emit("vector", lambda v: v.memset(eps_ap, EPS))
        P.emit("vector", lambda v: v.memset(yext_all[64:65, :], 1.0))
        def halo_zero(v):
            for c in range(DC):
                v.memset(xbig3[:, c, DB - 1:DB], 0.0)
                li = v.memset(xbig3[:, c, DB + T:DB + T + 1], 0.0)
            return li
        P.emit("vector", lambda v: s_v.inc(halo_zero(v)))
        halos_zeroed = s_v.n

        P.wait("vector", s_ldc, ldc_done)
        P.emit("vector", lambda v: s_v.inc(v.tensor_copy(
            B("CBF")[:, 0:8], vfm[:, VC["norm_b"]:VC["norm_b"] + 8])))
        nb_bf_done = s_v.n
        P.emit("vector", lambda v: v.tensor_scalar(
            F("OMU")[:, 0:8], vfm[:, VC["mur"]:VC["mur"] + 8], -1.0, 1.0,
            op0=ALU.mult, op1=ALU.add))
        P.emit("vector", lambda v: s_v.inc(v.tensor_scalar(
            F("OMU")[:, 8:16], vfm[:, VC["muk"]:VC["muk"] + 8], -1.0, 1.0,
            op0=ALU.mult, op1=ALU.add)))
        omu_done = s_v.n

        def omur(c):
            return F("OMU")[:, c:c + 1]

        def omuk(c):
            return F("OMU")[:, 8 + c:8 + c + 1]

        P.wait("scalar", s_ldc, ldc_done)
        P.emit("scalar", lambda s: s_a.inc(s.activation(
            scr1[:, 0:1], v64[:, 3:4], AF.Exp, bias=0.0, scale=1.0)))
        a_exp_done = s_a.n
        P.wait("vector", s_a, a_exp_done)
        P.emit("vector", lambda v: v.tensor_scalar_mul(a_ap, scr1[:, 0:1], -1.0))
        P.emit("vector", lambda v: s_v.inc(
            v.tensor_scalar_mul(a4_ap, scr1[:, 0:1], -1e-4)))
        a_done = s_v.n

        stage = [stg[:, i * 2048:(i + 1) * 2048] for i in range(2)]
        hb_vals, sc_vals = [], []
        for k in range(DC):
            par = k % 2
            if k >= 2:
                P.wait("gpsimd", s_pe, hb_vals[k - 2])
                P.wait("gpsimd", s_v, sc_vals[k - 2])
            P.emit("gpsimd", lambda g, par=par, k=k: s_wgp[par].inc(
                g.dma_start(out=stage[par], in_=in_wT[k * 128:(k + 1) * 128, :]), 16))
            ld = s_wgp[par].n
            P.wait("tensor", s_wgp[par], ld)
            if k == 0:
                P.wait("tensor", s_ldc, ldc_done)
                P.wait("tensor", s_v, nb_bf_done)

            def hb_mms(t, k=k, par=par):
                for n in range(4):
                    mm = t.matmul(ps[4 + n][0:1, :],
                                  B("CBF")[:, k:k + 1],
                                  stage[par][:, n * 512:(n + 1) * 512],
                                  start=(k == 0), stop=(k == DC - 1))
                    if n == 3:
                        s_pe.inc(mm)
            P.emit("tensor", hb_mms)
            hb_vals.append(s_pe.n)
            P.wait("vector", s_wgp[par], ld)
            P.emit("vector", lambda v, k=k, par=par: s_v.inc(v.tensor_scalar_mul(
                wa[:, k * 2048:(k + 1) * 2048], stage[par], vcol("norm_w", k))))
            sc_vals.append(s_v.n)
        wa_ready = s_v.n
        hb_done = s_pe.n
        hbrow = F("SCR")[0:1, 0:1024]
        P.wait("vector", s_pe, hb_done)
        for half in range(2):
            if half == 1:
                P.wait("vector", s_ldc, s_ldc.n)
            P.emit("vector", lambda v, half=half: v.tensor_copy(
                hbrow[:, 0:512], ps[4 + 2 * half][0:1, :]))
            P.emit("vector", lambda v, half=half: s_v.inc(v.tensor_copy(
                hbrow[:, 512:1024], ps[5 + 2 * half][0:1, :])))
            ev_done = s_v.n
            P.wait("sync", s_v, ev_done)
            P.emit("sync", lambda q, half=half: s_ldc.inc(q.dma_start(
                out=hb_sc[half * 1024:(half + 1) * 1024], in_=hbrow), 16))
        hb_in_hbm = s_ldc.n
        P.wait("sync", s_ldc, hb_in_hbm)
        P.emit("sync", lambda q: s_ldc.inc(q.dma_start(
            out=F("GTH")[:, 0:16],
            in_=hb_sc.rearrange("(m p) -> p m", p=128),
            allow_slow_non_contiguous=True), 16))
        hb_back = s_ldc.n
        P.wait("vector", s_ldc, hb_back)
        P.emit("vector", lambda v: s_v.inc(v.tensor_add(
            ibx, F("GTH")[:, 0:16], vfm[:, VC["in_b"]:VC["in_b"] + 16])))
        ibx_ready = s_v.n

        P.emit("gpsimd", lambda g: s_wc.inc(g.dma_start(
            out=wc.rearrange("p (c d) -> p c d", c=DC),
            in_=dtbc_wT.rearrange("(c p) d -> p c d", p=128)), 16))
        wc_ready = s_wc.n

        tap_dump("ibx", ibx, [(s_v, ibx_ready)])
        _maybe_stop("setup")

        pA = [ps[0], ps[1]]
        pdt, pb_, pc_ = ps[2], ps[3], ps[4]

        xn_done_tile = [0] * NTIL
        xnh_stored = [0] * NTIL
        x_wait = [0] * NTIL
        pe_mm_val = {}
        pe_dtbc_val = [0] * NG
        dt_chain_done = [0] * NG
        xnf_done_g = [0] * NG
        ev_base = s_ev.n
        ng_run = min(NG, _P1A_GROUPS)

        def emit_xload(gi, gate_sem=None, gate_val=0):
            if gate_sem is not None and gate_val > 0:
                P.wait("sync", gate_sem, gate_val)
            P.emit("sync", lambda q, gi=gi: s_xp[gi % 2].inc(q.dma_start(
                out=xt[gi % 2], in_=x_h[gi * 128:(gi + 1) * 128, :]), 16))
            x_wait[gi] = s_xp[gi % 2].n

        emit_xload(0)
        emit_xload(1)

        for g in range(ng_run):
            gs, ge = g * GT, (g + 1) * GT
            for t in range(4):
                gi = 4 * g + t
                P.wait("vector", s_xp[gi % 2], x_wait[gi])
                P.emit("vector", lambda v, gi=gi: ln_stats_tile(v, xt[gi % 2]))
                P.emit("vector", lambda v: s_v.inc(ln_agg_tile(v, mu_ap, var_ap)))
                ln1_done = s_v.n
                if gi == 0 and "lnst0" in taps:
                    def snap0(v):
                        v.tensor_copy(F("STF")[:, 0:12], st6[:, 0:12])
                        v.tensor_copy(F("STF")[:, 12:13], mu_ap)
                        return v.tensor_copy(F("STF")[:, 13:14], var_ap)
                    P.emit("vector", lambda v: s_v.inc(snap0(v)))
                    snap_done = s_v.n
                    P.wait("sync", s_v, snap_done)
                    P.emit("sync", lambda q: s_tap.inc(q.dma_start(
                        out=taps["lnst0"], in_=F("STF")[:, 0:16]), 16))
                P.wait("scalar", s_v, ln1_done)
                P.emit("scalar", lambda s, gi=gi: s_a.inc(
                    rsqrt_chain(s, var_ap, r1s[:, gi:gi + 1])))
                r1_done = s_a.n
                P.wait("vector", s_a, r1_done)
                P.emit("vector", lambda v, gi=gi: s_v.inc(v.tensor_scalar(
                    m1s[:, gi:gi + 1], mu_ap, r1s[:, gi:gi + 1], -1.0,
                    op0=ALU.mult, op1=ALU.mult)))
                nm1_done = s_v.n
                P.wait("scalar", s_v, nm1_done)
                P.emit("scalar", lambda s, gi=gi: s_a.inc(s.activation(
                    ut, xt[gi % 2], AF.Identity,
                    bias=m1s[:, gi:gi + 1], scale=r1s[:, gi:gi + 1])))
                u_done = s_a.n
                P.wait("vector", s_a, u_done)
                P.emit("vector", lambda v: ln_stats_tile(v, ut))
                P.emit("vector", lambda v: s_v.inc(ln_agg_tile(v, mu_ap, var_ap)))
                ln2_done = s_v.n
                P.wait("scalar", s_v, ln2_done)
                P.emit("scalar", lambda s: s_a.inc(
                    rsqrt_chain(s, var_ap, r_ap)))
                r2_done = s_a.n
                P.wait("vector", s_a, r2_done)
                P.emit("vector", lambda v: s_v.inc(v.tensor_scalar(
                    nm_ap, mu_ap, r_ap, -1.0, op0=ALU.mult, op1=ALU.mult)))
                nm2_done = s_v.n
                P.wait("scalar", s_v, nm2_done)
                if gi >= 2:
                    P.wait("scalar", s_xnhp[gi % 2], xnh_stored[gi - 2])
                P.emit("scalar", lambda s, gi=gi: s_a.inc(s.activation(
                    xn_tm[gi % 2], ut, AF.Identity,
                    bias=nm_ap, scale=r_ap)))
                xn_done_tile[gi] = s_a.n
                P.wait("sync", s_a, xn_done_tile[gi])
                P.emit("sync", lambda q, gi=gi: s_xnhp[gi % 2].inc(q.dma_start(
                    out=xn_sc[gi * 128:(gi + 1) * 128, :], in_=xn_tm[gi % 2]), 16))
                xnh_stored[gi] = s_xnhp[gi % 2].n
                if gi + 2 < NTIL:
                    emit_xload(gi + 2)

            _maybe_stop("p1a_ln%d" % g)
            if g >= 2:
                P.wait("sync", s_pe, pe_dtbc_val[g - 2])
            P.wait("sync", s_xnhp[0], s_xnhp[0].n)
            P.wait("sync", s_xnhp[1], s_xnhp[1].n)
            for c in range(DC):
                P.emit("sync", lambda q, g=g, c=c, gs=gs, ge=ge: s_xnf.inc(
                    q.dma_start_transpose(
                        xnf[g % 2][:, c * GT:(c + 1) * GT],
                        xn_sc[gs:ge, c * 128:(c + 1) * 128]), 16))
            xnf_done_g[g] = s_xnf.n

            if g == 0:
                tap_dump("xn0", xnf[0], [(s_xnf, xnf_done_g[0])])

            _maybe_stop("p1a_tr%d" % g)
            P.wait("tensor", s_xnf, xnf_done_g[g])
            if g == 0:
                P.wait("tensor", s_v, wa_ready)
                P.wait("tensor", s_wc, wc_ready)
            for m in range(_P1A_M):
                need = ev_base + _P1A_M * g + m - 1
                if need > ev_base and not _SKIP_EVAC:
                    P.wait("tensor", s_ev, need)

                def mm_in(tt, g=g, m=m):
                    for k in range(DC):
                        mm = tt.matmul(pA[m % 2][:, :],
                                       wa[:, k * 2048 + m * 128:
                                          k * 2048 + (m + 1) * 128],
                                       xnf[g % 2][:, k * GT:(k + 1) * GT],
                                       start=(k == 0), stop=(k == DC - 1))
                        if k == DC - 1:
                            s_pe.inc(mm)
                P.emit("tensor", mm_in)
                pe_mm_val[(g, m)] = s_pe.n
            P.wait("tensor", s_ev, ev_base + _P1A_M * g + _P1A_M)
            if g >= 1 and not _SKIP_DTBC:
                P.wait("tensor", s_a, dt_chain_done[g - 1])

            def mm_dtbc(tt, g=g):
                for bi, pp in enumerate((pdt, pb_, pc_)):
                    for k in range(DC):
                        mm = tt.matmul(pp[0:64, :],
                                       wc[:, k * 192 + bi * 64:
                                          k * 192 + (bi + 1) * 64],
                                       zb[:, k * GT:(k + 1) * GT],
                                       start=(k == 0), stop=(k == DC - 1))
                        if bi == 2 and k == DC - 1:
                            s_pe.inc(mm)
            P.emit("tensor", mm_dtbc)
            pe_dtbc_val[g] = s_pe.n

            _maybe_stop("p1a_mm%d" % g)
            if _SKIP_EVAC:
                P.wait("scalar", s_pe, pe_mm_val[(g, 15)])
                dt_chain_done[g] = s_a.n
                continue
            if g == 0:
                P.wait("scalar", s_v, ibx_ready)
            for m in range(_P1A_M):
                P.wait("scalar", s_pe, pe_mm_val[(g, m)])
                if m < 8:
                    dst = xbig3[:, m, DB + gs:DB + ge]
                else:
                    dst = zb[:, (m - 8) * GT:(m - 7) * GT]
                P.emit("scalar", lambda s, m=m, dst=dst: s_ev.inc(s.activation(
                    dst, pA[m % 2][:, :], AF.Identity,
                    bias=ibx[:, m:m + 1], scale=1.0)))
            if _SKIP_DTBC:
                dt_chain_done[g] = s_a.n
                continue
            P.wait("scalar", s_pe, pe_dtbc_val[g])
            if g == 0:
                P.wait("scalar", s_v, a_done)
            P.emit("scalar", lambda s: s.activation(
                scr1, pdt[0:64, :], AF.Exp, bias=v64[:, 0:1], scale=1.0))
            P.emit("scalar", lambda s: s.activation(
                scr2, scr1, AF.Ln, bias=1.0, scale=1.0))
            P.emit("scalar", lambda s, gs=gs, ge=ge: s.activation(
                decay[:, gs:ge], scr2, AF.Exp, bias=a4_ap, scale=a_ap))
            P.emit("scalar", lambda s, gs=gs, ge=ge: s.activation(
                rawb[:, gs:ge], pb_[0:64, :], AF.Identity,
                bias=v64[:, 1:2], scale=1.0))
            P.emit("scalar", lambda s, gs=gs, ge=ge: s_a.inc(s.activation(
                rawc[:, gs:ge], pc_[0:64, :], AF.Identity,
                bias=v64[:, 2:3], scale=1.0)))
            dt_chain_done[g] = s_a.n

        p1a_act_done = s_a.n
        _maybe_stop("p1a")
        tap_dump("xproj", B("XBIG"), [(s_a, p1a_act_done)])
        tap_dump("zdbg", zb, [(s_a, p1a_act_done)])
        tap_dump("r1s", r1s, [(s_a, p1a_act_done)])

        P.emit("scalar", lambda s: s.activation(bhat, rawb, AF.Tanh,
                                                bias=0.0, scale=1.0))
        P.emit("scalar", lambda s: s_a.inc(s.activation(
            chat, rawc, AF.Tanh, bias=0.0, scale=1.0)))
        tanh_bc_done = s_a.n

        P.wait("vector", s_a, tanh_bc_done)
        P.emit("vector", lambda v: s_v.inc(v.tensor_tensor_scan(
            states, decay, bhat, 0.0, op0=ALU.mult, op1=ALU.add)))
        scan1_done = s_v.n

        bnd = F("BND")[:, 0:17]
        P.emit("vector", lambda v: v.memset(bnd, 0.0))
        def bnd_fill(v):
            for c in range(DC):
                v.tensor_copy(bnd[:, c:c + 1], xbig3[:, c, DB:DB + 1])
                v.tensor_copy(bnd[:, 8 + c:9 + c],
                              xbig3[:, c, DB + T - 1:DB + T])
        P.emit("vector", bnd_fill)
        P.emit("vector", lambda v: s_v.inc(v.tensor_copy(
            bnd[0:64, 16:17], states[:, T - 1:T])))
        bnd_done = s_v.n
        P.wait("sync", s_v, bnd_done)
        P.emit("sync", lambda q: s_cc.inc(q.dma_start(out=cc1_in, in_=bnd), 16))
        cc1_staged = s_cc.n
        P.wait("gpsimd", s_cc, cc1_staged)
        P.emit("gpsimd", lambda g: s_g.inc(g.collective_compute(
            "AllGather", ALU.bypass, replica_groups=[list(range(8))],
            ins=[cc1_in], outs=[cc1_out])))
        cc1_done = s_g.n
        P.wait("sync", s_g, cc1_done)
        P.emit("sync", lambda q: s_cc.inc(q.dma_start(
            out=F("GTH").rearrange("p (r k) -> p r k", r=8),
            in_=cc1_out.rearrange("r p k -> p r k")), 16))
        gth1_done = s_cc.n
        tap_dump("gath1", F("GTH"), [(s_cc, gth1_done)])

        cw0 = [vcol("cw0", c) for c in range(DC)]
        cw1 = [vcol("cw1", c) for c in range(DC)]
        cw2 = [vcol("cw2", c) for c in range(DC)]
        cpre = [xnfm[:, i * DC * GT:(i + 1) * DC * GT] for i in range(2)]
        sv0 = F("BND")[:, 18:26]
        sv1 = F("BND")[:, 26:34]
        taps_done = [0] * NG
        tanh_done = [0] * NG
        for g in range(NG):
            gs, ge = g * GT, (g + 1) * GT
            if g == 0:
                P.wait("vector", s_a, p1a_act_done)
                P.wait("vector", s_v, halos_zeroed)
            if g >= 2:
                P.wait("vector", s_a, tanh_done[g - 2])

            def conv_taps(v, g=g, gs=gs, ge=ge):
                buf = cpre[g % 2]
                for c in range(DC):
                    b = buf[:, c * GT:(c + 1) * GT]
                    v.tensor_scalar_mul(b, xbig3[:, c, DB - 1 + gs:DB - 1 + ge],
                                        cw0[c])
                    v.scalar_tensor_tensor(b, xbig3[:, c, DB + gs:DB + ge],
                                           cw1[c], b, op0=ALU.mult, op1=ALU.add)
                    v.scalar_tensor_tensor(b, xbig3[:, c, DB + 1 + gs:DB + 1 + ge],
                                           cw2[c], b, op0=ALU.mult, op1=ALU.add)
            P.emit("vector", conv_taps)
            if g == 0:
                def sv0_fill(v):
                    for c in range(DC):
                        li = v.tensor_copy(sv0[:, c:c + 1],
                                           cpre[0][:, c * GT:c * GT + 1])
                    return li
                P.emit("vector", lambda v: s_v.inc(sv0_fill(v)))
            elif g == NG - 1:
                def sv1_fill(v):
                    for c in range(DC):
                        li = v.tensor_copy(
                            sv1[:, c:c + 1],
                            cpre[(NG - 1) % 2][:, c * GT + GT - 1:c * GT + GT])
                    return li
                P.emit("vector", lambda v: s_v.inc(sv1_fill(v)))
            else:
                P.emit("vector", lambda v, g=g: s_v.inc(v.tensor_copy(
                    agg[:, 26:27], cpre[g % 2][:, 0:1])))
            taps_done[g] = s_v.n
            if g >= 1:
                gp = g - 1
                P.wait("scalar", s_v, taps_done[g])

                def tanh_g(s, gp=gp):
                    for c in range(DC):
                        ai = s.activation(
                            xbig3[:, c, DB + gp * GT:DB + (gp + 1) * GT],
                            cpre[gp % 2][:, c * GT:(c + 1) * GT],
                            AF.Tanh, bias=vcol("cb", c), scale=1.0)
                        if c == DC - 1:
                            s_a.inc(ai)
                P.emit("scalar", tanh_g)
                tanh_done[gp] = s_a.n
        P.wait("scalar", s_v, taps_done[NG - 1])

        def tanh_last(s):
            gp = NG - 1
            for c in range(DC):
                ai = s.activation(xbig3[:, c, DB + gp * GT:DB + (gp + 1) * GT],
                                  cpre[gp % 2][:, c * GT:(c + 1) * GT],
                                  AF.Tanh, bias=vcol("cb", c), scale=1.0)
                if c == DC - 1:
                    s_a.inc(ai)
        P.emit("scalar", tanh_last)
        tanh_done[NG - 1] = s_a.n

        acc = [F("ACC")[:, 0:17], F("ACC")[:, 17:34]]
        sel = vfm[:, VC["sel"]:VC["sel"] + 8]
        g_left = vfm[:, VC["gates"]:VC["gates"] + 1]
        g_right = vfm[:, VC["gates"] + 1:VC["gates"] + 2]
        gth = F("GTH")
        P.wait("vector", s_cc, gth1_done)
        P.wait("vector", s_a, tanh_done[NG - 1])

        PR = [F("ACC")[:, i * 17:(i + 1) * 17] for i in range(8)]

        def select_tree(v, width):
            for r in range(8):
                v.tensor_scalar_mul(PR[r][:, 0:width],
                                    gth[:, r * width:(r + 1) * width],
                                    sel[:, r:r + 1])
            for r in range(4):
                v.tensor_add(PR[2 * r][:, 0:width], PR[2 * r][:, 0:width],
                             PR[2 * r + 1][:, 0:width])
            v.tensor_add(PR[0][:, 0:width], PR[0][:, 0:width],
                         PR[2][:, 0:width])
            v.tensor_add(PR[4][:, 0:width], PR[4][:, 0:width],
                         PR[6][:, 0:width])
            v.tensor_copy(PR[3][:, 0:width], PR[2][:, 0:width])
            v.tensor_add(PR[1][:, 0:width], PR[0][:, 0:width],
                         PR[4][:, 0:width])
            return v.tensor_copy(PR[2][:, 0:width], PR[0][:, 0:width])
        P.emit("vector", lambda v: select_tree(v, 17))
        recv = PR[1]
        def halo_fill(v):
            for c in range(DC):
                v.tensor_scalar_mul(xbig3[:, c, DB - 1:DB],
                                    recv[:, 8 + c:9 + c], g_left)
                v.tensor_scalar_mul(xbig3[:, c, DB + T:DB + T + 1],
                                    recv[:, c:c + 1], g_right)
        P.emit("vector", halo_fill)
        carry = F("BND")[0:64, 34:35]
        P.emit("vector", lambda v: s_v.inc(v.tensor_scalar_mul(
            carry, recv[0:64, 16:17], g_left[0:64, :])))
        ex1_done = s_v.n

        fix0 = F("GTH")[:, 0:8]
        fix1 = F("GTH")[:, 8:16]

        def fixup_pre(v):
            for c in range(DC):
                v.scalar_tensor_tensor(fix0[:, c:c + 1],
                                       xbig3[:, c, DB - 1:DB],
                                       cw0[c], sv0[:, c:c + 1],
                                       op0=ALU.mult, op1=ALU.add)
                li = v.scalar_tensor_tensor(fix1[:, c:c + 1],
                                            xbig3[:, c, DB + T:DB + T + 1],
                                            cw2[c], sv1[:, c:c + 1],
                                            op0=ALU.mult, op1=ALU.add)
            return li
        def fixup_pre_inc(v):
            s_v.inc(fixup_pre(v))
        P.emit("vector", fixup_pre_inc)
        fixpre_done = s_v.n
        P.wait("scalar", s_v, fixpre_done)

        def fixup_tanh(s):
            for c in range(DC):
                s.activation(xbig3[:, c, DB:DB + 1], fix0[:, c:c + 1], AF.Tanh,
                             bias=vcol("cb", c), scale=1.0)
                ai = s.activation(xbig3[:, c, DB + T - 1:DB + T],
                                  fix1[:, c:c + 1],
                                  AF.Tanh, bias=vcol("cb", c), scale=1.0)
                if c == DC - 1:
                    s_a.inc(ai)
        P.emit("scalar", fixup_tanh)
        conv_all_done = s_a.n

        P.wait("vector", s_v, ex1_done)
        P.emit("vector", lambda v: v.tensor_tensor_scan(
            states, decay, bhat, carry[:, 0:1], op0=ALU.mult, op1=ALU.add))
        yext = yext_all[0:64, :]
        P.emit("vector", lambda v: s_v.inc(v.tensor_mul(yext, chat, states)))
        y_done = s_v.n

        for nm, apv, sm, vv in (("decay", decay, s_a, tanh_bc_done),
                                ("bhat", bhat, s_a, tanh_bc_done),
                                ("chat", chat, s_a, tanh_bc_done),
                                ("states", states, s_v, y_done),
                                ("y", yext, s_v, y_done)):
            tap_dump(nm, apv, [(sm, vv)])
        tap_dump("xconv", B("XBIG"), [(s_a, conv_all_done), (s_v, y_done)])

        owt_stage = [bscr[:, i * 1024:(i + 1) * 1024] for i in range(2)]
        P.wait("gpsimd", s_v, y_done)
        P.wait("gpsimd", s_a, tanh_bc_done)
        owt_scaled = [0] * 9
        for k in range(9):
            par = k % 2
            if k >= 2:
                P.wait("gpsimd", s_v, owt_scaled[k - 2])
            if k < 8:
                P.emit("gpsimd", lambda g, k=k, par=par: s_wgp[par].inc(g.dma_start(
                    out=owt_stage[par], in_=out_wT[k * 128:(k + 1) * 128, :]), 16))
            else:
                P.emit("gpsimd", lambda g, par=par: s_wgp[par].inc(g.dma_start(
                    out=owt_stage[par][0:65, :], in_=out_wT[1024:1089, :]), 16))
            ldv = s_wgp[par].n
            P.wait("vector", s_wgp[par], ldv)
            if k < 8:
                P.emit("vector", lambda v, k=k, par=par: s_v.inc(v.tensor_scalar_mul(
                    wb[:, k * 1024:(k + 1) * 1024], owt_stage[par], vcol("dvec", k))))
            else:
                P.emit("vector", lambda v, par=par: s_v.inc(v.tensor_copy(
                    wb[0:65, 8 * 1024:9 * 1024], owt_stage[par][0:65, :])))
            owt_scaled[k] = s_v.n
        owt_ready = s_v.n
        _maybe_stop("p1c")

        vtm = [d1[:, i * D:(i + 1) * D] for i in range(2)]
        x2_wait = [0] * NTIL
        vtm_done = [0] * NTIL
        vtm_stored = [0] * NTIL
        psfree_val = [0] * NTIL
        lnAap_tile = [0] * NTIL
        xm_done_tile = [0] * NTIL
        xm_stored = [0] * NTIL
        pe2_val = [0] * NTIL
        pe2_group = [0] * NG
        vf_done_g = [0] * NG

        def emit_x2load(gi):
            P.emit("sync", lambda q, gi=gi: s_xp[gi % 2].inc(q.dma_start(
                out=xt[gi % 2], in_=x_h[gi * 128:(gi + 1) * 128, :]), 16))
            x2_wait[gi] = s_xp[gi % 2].n

        emit_x2load(0)
        emit_x2load(1)

        for g in range(NG):
            gs = g * GT
            for t in range(4):
                gi = 4 * g + t
                tw0 = gs + t * 128
                pn = [ps[2 * (gi % 2)], ps[2 * (gi % 2) + 1]]

                if gi == 0:
                    P.wait("tensor", s_v, owt_ready)
                    P.wait("tensor", s_a, conv_all_done)
                    P.wait("tensor", s_v, y_done)
                if gi >= 2:
                    P.wait("tensor", s_v, psfree_val[gi - 2])

                def mm_out(tt, tw0=tw0, pn=pn):
                    for k in range(9):
                        if k < 8:
                            lhs = xbig3[:, k, DB + tw0:DB + tw0 + 128]
                            r0, r1_ = 0, 128
                        else:
                            lhs = yext_all[0:65, tw0:tw0 + 128]
                            r0, r1_ = 0, 65
                        for n in range(2):
                            mm = tt.matmul(pn[n][:, :], lhs,
                                           wb[r0:r1_, k * 1024 + n * 512:
                                              k * 1024 + (n + 1) * 512],
                                           start=(k == 0), stop=(k == 8))
                            if k == 8 and n == 1:
                                s_pe.inc(mm)
                P.emit("tensor", mm_out)
                pe2_val[gi] = s_pe.n
                if t == 3:
                    pe2_group[g] = s_pe.n

                P.wait("scalar", s_xp[gi % 2], x2_wait[gi])
                if gi >= 1:
                    P.wait("scalar", s_v, xm_done_tile[gi - 1])
                P.emit("scalar", lambda s, gi=gi: s_a.inc(s.activation(
                    ut, xt[gi % 2], AF.Identity,
                    bias=m1s[:, gi:gi + 1], scale=r1s[:, gi:gi + 1])))
                u2_done = s_a.n
                P.wait("vector", s_pe, pe2_val[gi])
                P.wait("vector", s_a, u2_done)
                if gi >= 2:
                    P.wait("vector", s_a, lnAap_tile[gi - 2])
                P.emit("vector", lambda v, gi=gi, pn=pn: v.tensor_add(
                    att[gi % 2][:, 0:512], pn[0][:, :], ut[:, 0:512]))
                P.emit("vector", lambda v, gi=gi, pn=pn: s_v.inc(v.tensor_add(
                    att[gi % 2][:, 512:1024], pn[1][:, :], ut[:, 512:1024])))
                psfree_val[gi] = s_v.n
                if gi == 0 and "att0" in taps:
                    P.wait("sync", s_v, psfree_val[0])
                    P.emit("sync", lambda q: s_tap.inc(q.dma_start(
                        out=taps["att0"], in_=att[0]), 16))
                P.emit("vector", lambda v, gi=gi: ln_stats_tile(v, att[gi % 2]))
                P.emit("vector", lambda v: s_v.inc(ln_agg_tile(v, mu_ap, var_ap)))
                lnA_done = s_v.n
                P.wait("scalar", s_v, lnA_done)
                P.emit("scalar", lambda s: s_a.inc(
                    rsqrt_chain(s, var_ap, r_ap)))
                rA_done = s_a.n
                P.wait("vector", s_a, rA_done)
                P.emit("vector", lambda v: s_v.inc(v.tensor_scalar(
                    nm_ap, mu_ap, r_ap, -1.0, op0=ALU.mult, op1=ALU.mult)))
                nmA_done = s_v.n
                P.wait("scalar", s_v, nmA_done)
                P.emit("scalar", lambda s, gi=gi: s_a.inc(s.activation(
                    ut, att[gi % 2], AF.Identity,
                    bias=nm_ap, scale=r_ap)))
                lnAap_tile[gi] = s_a.n
                P.wait("vector", s_a, lnAap_tile[gi])
                if gi >= 2:
                    P.wait("vector", s_a, vtm_done[gi - 2])
                    P.wait("vector", s_xmp[gi % 2], xm_stored[gi - 2])
                P.emit("vector", lambda v, gi=gi: s_v.inc(v.tensor_add(
                    xmr[gi % 2], xt[gi % 2], ut)))
                xm_done_tile[gi] = s_v.n
                if gi == 0 and "xm0" in taps:
                    P.wait("sync", s_v, xm_done_tile[0])
                    P.emit("sync", lambda q: s_tap.inc(q.dma_start(
                        out=taps["xm0"], in_=xmr[0]), 16))
                P.wait("sync", s_v, xm_done_tile[gi])
                P.emit("sync", lambda q, gi=gi: s_xmp[gi % 2].inc(q.dma_start(
                    out=out_h[gi * 128:(gi + 1) * 128, :], in_=xmr[gi % 2]), 16))
                xm_stored[gi] = s_xmp[gi % 2].n
                P.emit("vector", lambda v, gi=gi: ln_stats_tile(v, xmr[gi % 2]))
                P.emit("vector", lambda v: s_v.inc(ln_agg_tile(v, mu_ap, var_ap)))
                ln3_done = s_v.n
                P.wait("scalar", s_v, ln3_done)
                P.emit("scalar", lambda s: s_a.inc(
                    rsqrt_chain(s, var_ap, r_ap)))
                r3_done = s_a.n
                P.wait("vector", s_a, r3_done)
                P.emit("vector", lambda v: s_v.inc(v.tensor_scalar(
                    nm_ap, mu_ap, r_ap, -1.0, op0=ALU.mult, op1=ALU.mult)))
                nm3_done = s_v.n
                P.wait("scalar", s_v, nm3_done)
                if gi >= 2:
                    P.wait("scalar", s_vhp[gi % 2], vtm_stored[gi - 2])
                P.emit("scalar", lambda s, gi=gi: s_a.inc(s.activation(
                    vtm[gi % 2], xmr[gi % 2], AF.Identity,
                    bias=nm_ap, scale=r_ap)))
                vtm_done[gi] = s_a.n
                P.wait("sync", s_a, vtm_done[gi])
                P.emit("sync", lambda q, gi=gi: s_vhp[gi % 2].inc(q.dma_start(
                    out=v_sc[gi * 128:(gi + 1) * 128, :], in_=vtm[gi % 2]), 16))
                vtm_stored[gi] = s_vhp[gi % 2].n
                if gi + 2 < NTIL:
                    emit_x2load(gi + 2)

            P.wait("sync", s_pe, pe2_group[g])
            P.wait("sync", s_vhp[0], s_vhp[0].n)
            P.wait("sync", s_vhp[1], s_vhp[1].n)
            for c in range(DC):
                P.emit("sync", lambda q, c=c, gs=gs: s_vf.inc(
                    q.dma_start_transpose(
                        B("XBIG")[:, c * CW + DB + gs:c * CW + DB + gs + GT],
                        v_sc[gs:gs + GT, c * 128:(c + 1) * 128]), 16))
            vf_done_g[g] = s_vf.n

        if "vsc" in taps:
            P.wait("sync", s_vhp[0], s_vhp[0].n)
            P.wait("sync", s_vhp[1], s_vhp[1].n)
            P.emit("sync", lambda q: s_tap.inc(
                q.dma_start(out=taps["vsc"], in_=v_sc), 16))
        P.wait("gpsimd", s_pe, pe2_group[0])
        P.emit("gpsimd", lambda g: s_wgp[0].inc(g.dma_start(
            out=wa[:, 0:8192].rearrange("p (c d) -> p c d", c=DC),
            in_=wr_h.rearrange("(c p) d -> p c d", p=128)), 16))
        P.emit("gpsimd", lambda g: s_wgp[1].inc(g.dma_start(
            out=wa[:, 8192:16384].rearrange("p (c d) -> p c d", c=DC),
            in_=wk_h.rearrange("(c p) d -> p c d", p=128)), 16))
        wrk_ready = (s_wgp[0].n, s_wgp[1].n)

        P.wait("gpsimd", s_pe, pe2_group[NG - 1])
        P.emit("gpsimd", lambda g: s_wc.inc(g.dma_start(
            out=wb[:, 0:8192].rearrange("p (c d) -> p c d", c=DC),
            in_=wv_h.rearrange("(c p) d -> p c d", p=128)), 16))
        wv_ready = s_wc.n

        bnd2 = F("BND")[:, 0:8]
        P.wait("vector", s_vf, vf_done_g[NG - 1])
        def bnd2_fill(v):
            for c in range(DC):
                li = v.tensor_copy(bnd2[:, c:c + 1],
                                   xbig3[:, c, DB + T - 1:DB + T])
            return li
        P.emit("vector", lambda v: s_v.inc(bnd2_fill(v)))
        bnd2_done = s_v.n
        P.wait("sync", s_v, bnd2_done)
        P.emit("sync", lambda q: s_cc.inc(q.dma_start(out=cc2_in, in_=bnd2), 16))
        cc2_staged = s_cc.n
        P.wait("gpsimd", s_cc, cc2_staged)
        P.emit("gpsimd", lambda g: s_g.inc(g.collective_compute(
            "AllGather", ALU.bypass, replica_groups=[list(range(8))],
            ins=[cc2_in], outs=[cc2_out])))
        cc2_done = s_g.n
        P.wait("sync", s_g, cc2_done)
        P.emit("sync", lambda q: s_cc.inc(q.dma_start(
            out=F("GTH")[:, 0:64].rearrange("p (r k) -> p r k", r=8),
            in_=cc2_out.rearrange("r p k -> p r k")), 16))
        gth2_done = s_cc.n
        P.wait("vector", s_cc, gth2_done)

        P.emit("vector", lambda v: select_tree(v, 8))
        def vprev_fill(v):
            for c in range(DC):
                li = v.tensor_scalar_mul(xbig3[:, c, DB - 1:DB],
                                         recv[:, c:c + 1], g_left)
            return li
        P.emit("vector", lambda v: s_v.inc(vprev_fill(v)))
        ex2_done = s_v.n
        tap_dump("vfm_dbg", B("XBIG"), [(s_v, ex2_done)])
        _maybe_stop("p2a")

        xrb = xnfm[:, 0:DC * GT]
        xkb = xnfm[:, DC * GT:2 * DC * GT]
        s1b = bscr[:, 0:512]
        krel = [bscr[:, 512 + i * 512:512 + (i + 1) * 512] for i in range(2)]
        rr = [bscr[:, 1536 + i * D:1536 + (i + 1) * D] for i in range(2)]
        accsc = bscr[:, 3584:4608]
        kact = B("YEXT")
        ffsl = [stg, d1]
        st6f_mu = F("STF")[:, 0:8]
        st6f_var = F("STF")[:, 8:16]
        rph = F("RPH")
        mur = [vcol("mur", c) for c in range(DC)]
        muk = [vcol("muk", c) for c in range(DC)]

        out_stored = [0] * NTIL
        xm_reread = [0] * NTIL
        pe_k_val = {}
        pe_r_val = {}
        pe_v_val = {}
        rr_done = {}
        krel_val = {}
        kact_done = {}
        dve_ffn_val = {}
        acc_done = {}
        p2d_outv_last = 0
        nmphi_of = {}

        def emit_p2d_read(gi):
            if gi >= 2:
                P.wait("sync", s_finp[gi % 2], out_stored[gi - 2])
            P.emit("sync", lambda q, gi=gi: s_xrp[gi % 2].inc(q.dma_start(
                out=xt[gi % 2], in_=out_h[gi * 128:(gi + 1) * 128, :]), 16))
            xm_reread[gi] = s_xrp[gi % 2].n

        def emit_p2d_tile(psrc, j):
            gi = 8 * psrc + j
            gl_ = j // 4
            fsrc = ffsl[gl_][:, (j % 4) * D:(j % 4 + 1) * D]
            P.wait("vector", s_v, nmphi_of[psrc])
            P.wait("vector", s_xrp[gi % 2], xm_reread[gi])
            if gi >= 2:
                P.wait("vector", s_finp[gi % 2], out_stored[gi - 2])
            P.emit("vector", lambda v, fsrc=fsrc, j=j, gi=gi: v.tensor_scalar(
                xmr[gi % 2], fsrc, st6f_mu[:, j:j + 1], rph[:, j:j + 1],
                op0=ALU.subtract, op1=ALU.mult))
            P.emit("vector", lambda v, gi=gi: s_v.inc(v.tensor_add(
                att[gi % 2], xmr[gi % 2], xt[gi % 2])))
            outv = s_v.n
            P.wait("sync", s_v, outv)
            P.emit("sync", lambda q, gi=gi: s_finp[gi % 2].inc(q.dma_start(
                out=out_h[gi * 128:(gi + 1) * 128, :], in_=att[gi % 2]), 16))
            out_stored[gi] = s_finp[gi % 2].n
            return outv

        for p in range(4):
            for gl, g in enumerate((2 * p, 2 * p + 1)):
                gs = g * GT
                if g == 0:
                    P.wait("vector", s_v, ex2_done)
                    P.wait("vector", s_v, omu_done)
                else:
                    P.wait("vector", s_vf, vf_done_g[g])
                    P.wait("vector", s_pe, pe_v_val[4 * (g - 1) + 3])

                def mk_xrk(v, g=g, gs=gs):
                    for c in range(DC):
                        vwin = xbig3[:, c, DB + gs:DB + gs + GT]
                        vprev = xbig3[:, c, DB - 1 + gs:DB - 1 + gs + GT]
                        v.tensor_scalar_mul(s1b, vwin, mur[c])
                        v.scalar_tensor_tensor(xrb[:, c * GT:(c + 1) * GT],
                                               vprev, omur(c), s1b,
                                               op0=ALU.mult, op1=ALU.add)
                        v.tensor_scalar_mul(s1b, vwin, muk[c])
                        li = v.scalar_tensor_tensor(xkb[:, c * GT:(c + 1) * GT],
                                                    vprev, omuk(c), s1b,
                                                    op0=ALU.mult, op1=ALU.add)
                        if c == DC - 1:
                            s_v.inc(li)
                P.emit("vector", mk_xrk)
                xrk_done = s_v.n
                if g == 0:
                    tap_dump("xr0", xrb, [(s_v, xrk_done)])
                    tap_dump("xk0", xkb, [(s_v, xrk_done)])

                P.wait("tensor", s_v, xrk_done)
                if g == 0:
                    P.wait("tensor", s_wgp[0], wrk_ready[0])
                    P.wait("tensor", s_wgp[1], wrk_ready[1])
                for m in range(DC):
                    prev = (g, m - 2) if m >= 2 else (
                        (g - 1, DC - 2 + m) if g >= 1 else None)
                    if prev is not None and prev in krel_val:
                        P.wait("tensor", s_a, krel_val[prev])

                    def mm_k(tt, m=m):
                        pp = ps[4 + m % 2]
                        for k in range(DC):
                            mm = tt.matmul(pp[:, :],
                                           wa[:, 8192 + k * 1024 + m * 128:
                                              8192 + k * 1024 + (m + 1) * 128],
                                           xkb[:, k * GT:(k + 1) * GT],
                                           start=(k == 0), stop=(k == DC - 1))
                            if k == DC - 1:
                                s_pe.inc(mm)
                    P.emit("tensor", mm_k)
                    pe_k_val[(g, m)] = s_pe.n
                    P.wait("scalar", s_pe, pe_k_val[(g, m)])
                    if prev is not None and prev in kact_done:
                        P.wait("scalar", s_v, kact_done[prev])
                    P.emit("scalar", lambda s, m=m: s_a.inc(s.activation(
                        krel[m % 2], ps[4 + m % 2][:, :], AF.Relu,
                        bias=0.0, scale=1.0)))
                    krel_val[(g, m)] = s_a.n
                    P.wait("vector", s_a, krel_val[(g, m)])
                    if m == 0 and g >= 1:
                        P.wait("vector", s_pe, pe_v_val[4 * (g - 1) + 3])
                    P.emit("vector", lambda v, m=m: s_v.inc(v.tensor_mul(
                        kact[:, m * GT:(m + 1) * GT], krel[m % 2], krel[m % 2])))
                    kact_done[(g, m)] = s_v.n
                if g == 0:
                    tap_dump("kact0", kact, [(s_v, kact_done[(0, DC - 1)])])

                for t in range(4):
                    gi = 4 * g + t
                    jl = 4 * gl + t
                    if p >= 1:
                        pj = 8 * (p - 1) + jl
                        if jl == 0:
                            emit_p2d_read(pj)
                        if jl + 1 < 8:
                            emit_p2d_read(pj + 1)
                        ov = emit_p2d_tile(p - 1, jl)
                        if jl == 7:
                            p2d_outv_last = ov
                    if gi >= 1:
                        P.wait("tensor", s_a, rr_done[gi - 1])

                    def mm_r(tt, t=t):
                        for k in range(DC):
                            for n in range(2):
                                mm = tt.matmul(ps[n][:, :],
                                               xrb[:, k * GT + t * 128:
                                                   k * GT + (t + 1) * 128],
                                               wa[:, k * 1024 + n * 512:
                                                  k * 1024 + (n + 1) * 512],
                                               start=(k == 0),
                                               stop=(k == DC - 1))
                                if k == DC - 1 and n == 1:
                                    s_pe.inc(mm)
                    P.emit("tensor", mm_r)
                    pe_r_val[gi] = s_pe.n
                    P.wait("scalar", s_pe, pe_r_val[gi])
                    if gi >= 2:
                        P.wait("scalar", s_v, dve_ffn_val[gi - 2])
                    P.emit("scalar", lambda s, gi=gi: (
                        s.activation(rr[gi % 2][:, 0:512], ps[0][:, :], AF.Sigmoid,
                                     bias=0.0, scale=1.0),
                        s_a.inc(s.activation(rr[gi % 2][:, 512:1024], ps[1][:, :],
                                             AF.Sigmoid, bias=0.0, scale=1.0)))[1])
                    rr_done[gi] = s_a.n
                    P.wait("tensor", s_v, kact_done[(g, DC - 1)])
                    if gi == 0:
                        P.wait("tensor", s_wc, wv_ready)
                    if gi >= 1:
                        P.wait("tensor", s_v, dve_ffn_val[gi - 1])

                    def mm_v(tt, t=t):
                        for k in range(DC):
                            for n in range(2):
                                mm = tt.matmul(ps[2 + n][:, :],
                                               kact[:, k * GT + t * 128:
                                                    k * GT + (t + 1) * 128],
                                               wb[:, k * 1024 + n * 512:
                                                  k * 1024 + (n + 1) * 512],
                                               start=(k == 0),
                                               stop=(k == DC - 1))
                                if k == DC - 1 and n == 1:
                                    s_pe.inc(mm)
                    P.emit("tensor", mm_v)
                    pe_v_val[gi] = s_pe.n
                    P.wait("vector", s_pe, pe_v_val[gi])
                    P.wait("vector", s_a, rr_done[gi])
                    if gi >= 8:
                        P.wait("vector", s_a, acc_done[gi - 8])
                    fdst = ffsl[gl][:, t * D:(t + 1) * D]
                    P.emit("vector", lambda v, fdst=fdst, gi=gi: v.tensor_mul(
                        fdst[:, 0:512], ps[2][:, :], rr[gi % 2][:, 0:512]))
                    P.emit("vector", lambda v, fdst=fdst, gi=gi: s_v.inc(
                        v.tensor_mul(fdst[:, 512:1024], ps[3][:, :],
                                     rr[gi % 2][:, 512:1024])))
                    dve_ffn_val[gi] = s_v.n
                    P.wait("scalar", s_v, dve_ffn_val[gi])
                    P.emit("scalar", lambda s, fdst=fdst, jl=jl: s.activation(
                        accsc, fdst, AF.Identity, bias=0.0, scale=1.0,
                        accum_out=agg[:, jl:jl + 1]))
                    P.emit("scalar", lambda s, fdst=fdst, jl=jl: s_a.inc(
                        s.activation(accsc, fdst, AF.Square, bias=0.0,
                                     scale=1.0,
                                     accum_out=agg[:, 8 + jl:9 + jl])))
                    acc_done[gi] = s_a.n

            if p >= 1:
                P.wait("scalar", s_v, p2d_outv_last)
            P.emit("scalar", lambda s: s_a.inc(s.activation(
                st6f_mu, agg[:, 0:8], AF.Identity, bias=0.0,
                scale=1.0 / 1024.0)))
            mu_done = s_a.n
            P.wait("vector", s_a, mu_done)
            P.emit("vector", lambda v: v.tensor_mul(
                rph[:, 16:24], st6f_mu, st6f_mu))
            P.emit("vector", lambda v: v.tensor_copy(rph[:, 8:16], st6f_mu))
            P.emit("vector", lambda v: s_v.inc(v.scalar_tensor_tensor(
                st6f_var, agg[:, 8:16], 1.0 / 1024.0, rph[:, 16:24],
                op0=ALU.mult, op1=ALU.subtract)))
            var_done = s_v.n
            P.wait("scalar", s_v, var_done)
            P.emit("scalar", lambda s: s.activation(
                rph[:, 16:24], st6f_var, AF.Ln, bias=eps_ap, scale=1.0))
            P.emit("scalar", lambda s: s_a.inc(s.activation(
                rph[:, 0:8], rph[:, 16:24], AF.Exp, bias=0.0, scale=-0.5)))
            rphi_done = s_a.n
            P.wait("vector", s_a, rphi_done)
            P.emit("vector", lambda v: s_v.inc(v.tensor_copy(
                rph[:, 8:16], rph[:, 0:8])))
            nmphi_of[p] = s_v.n

        for j in range(8):
            pj = 24 + j
            if j == 0:
                emit_p2d_read(pj)
            if j + 1 < 8:
                emit_p2d_read(pj + 1)
            emit_p2d_tile(3, j)

        P.wait("sync", s_finp[0], s_finp[0].n)
        P.wait("sync", s_finp[1], s_finp[1].n)
        P.wait("sync", s_xmp[0], s_xmp[0].n)
        P.wait("sync", s_xmp[1], s_xmp[1].n)
        P.wait("sync", s_tap, s_tap.n)

    except _StopBuild:
        pass
    blk = ctx.enter_context(nc.Block())
    for eng_name, dec in (("sync", blk.sync), ("gpsimd", blk.gpsimd),
                          ("tensor", blk.tensor), ("vector", blk.vector),
                          ("scalar", blk.scalar)):
        def body(q, eng_name=eng_name):
            for fn in P.stream[eng_name]:
                fn(q)
        dec(body)
    ctx.close()
    return nc


def _host_inputs(inputs):
    x = np.ascontiguousarray(np.asarray(inputs["L0_attn_x"], dtype=np.float32))
    in_w = np.asarray(inputs["L0_attn_in_w"], np.float32)
    dt_w = np.asarray(inputs["L0_attn_dt_w"], np.float32)
    b_w = np.asarray(inputs["L0_attn_b_w"], np.float32)
    c_w = np.asarray(inputs["L0_attn_c_w"], np.float32)
    out_w = np.asarray(inputs["L0_attn_out_w"], np.float32)
    out_b = np.asarray(inputs["L0_attn_out_b"], np.float32)
    wr = np.asarray(inputs["L0_ffn_W_r"], np.float32)
    wk = np.asarray(inputs["L0_ffn_W_k"], np.float32)
    wv = np.asarray(inputs["L0_ffn_W_v"], np.float32)

    in_wT = np.ascontiguousarray(in_w.T)
    dtbc_wT = np.ascontiguousarray(np.concatenate([dt_w, b_w, c_w], 0).T)
    out_wT = np.ascontiguousarray(
        np.concatenate([out_w, out_b[:, None]], 1).T)
    wr_T = np.ascontiguousarray(wr.T)
    wk_T = np.ascontiguousarray(wk.T)
    wv_T = np.ascontiguousarray(wv.T)

    def fm(vec):
        return np.ascontiguousarray(
            np.asarray(vec, np.float32).reshape(-1, 128).T)

    conv_w = np.asarray(inputs["L0_attn_conv_w"], np.float32)[:, 0, :]
    vbase = np.zeros((128, 98), np.float32)
    vbase[:, 0:8] = fm(inputs["L0_attn_norm_w"])
    vbase[:, 8:16] = fm(inputs["L0_attn_norm_b"])
    vbase[:, 16:32] = fm(inputs["L0_attn_in_b"])
    vbase[:, 32:40] = fm(conv_w[:, 0])
    vbase[:, 40:48] = fm(conv_w[:, 1])
    vbase[:, 48:56] = fm(conv_w[:, 2])
    vbase[:, 56:64] = fm(inputs["L0_attn_conv_b"])
    vbase[:, 64:72] = fm(inputs["L0_attn_d"])
    vbase[:, 72:80] = fm(inputs["L0_ffn_mu_r"])
    vbase[:, 80:88] = fm(inputs["L0_ffn_mu_k"])

    v64 = np.zeros((S, 8), np.float32)
    v64[:, 0] = np.asarray(inputs["L0_attn_dt_b"], np.float32)
    v64[:, 1] = np.asarray(inputs["L0_attn_b_b"], np.float32)
    v64[:, 2] = np.asarray(inputs["L0_attn_c_b"], np.float32)
    v64[:, 3] = np.asarray(inputs["L0_attn_a_log"], np.float32)

    in_maps = []
    for c in range(8):
        b, h = c // 2, c % 2
        vfm = vbase.copy()
        vfm[:, 88 + (c ^ 1)] = 1.0
        vfm[:, 96] = 1.0 if h == 1 else 0.0
        vfm[:, 97] = 1.0 if h == 0 else 0.0
        in_maps.append({
            "x": np.ascontiguousarray(x[b, h * T:(h + 1) * T, :]),
            "in_wT": in_wT, "dtbc_wT": dtbc_wT, "out_wT": out_wT,
            "wr_T": wr_T, "wk_T": wk_T, "wv_T": wv_T,
            "vfm": vfm, "v64": v64,
        })
    return in_maps


def kernel(**inputs):
    nc = build_nc()
    in_maps = _host_inputs(inputs)
    res = run_bass_kernel_spmd(nc, in_maps, core_ids=list(range(8)))
    out = np.empty((4, 8192, 1024), np.float32)
    for c in range(8):
        b, h = c // 2, c % 2
        out[b, h * T:(h + 1) * T, :] = res.results[c]["out"]
    return out
